# version 01 (final)
# baseline (speedup 1.0000x reference)
import contextlib
import numpy as np
import ml_dtypes

import concourse.bass as bass
import concourse.mybir as mybir
from concourse.bass_utils import run_bass_kernel_spmd

F32 = mybir.dt.float32
BF16 = mybir.dt.bfloat16
AF = mybir.ActivationFunctionType
ALU = mybir.AluOpType
AX = mybir.AxisListType

NCORES = 8
D = 1024
DC = D // 128
NMB = 16
SEQ = 2048
HALO = 3

NT_A = 2 * SEQ
SB_B = 256 + HALO
NT_B = NMB * SB_B

LNEPS0 = 1e-5
LNEPS1 = 1e-6
BIG_NEG = -1e30


def tb_list(ntok, tb=512, off0=0):
    out, o = [], 0
    while o < ntok:
        l = min(tb, ntok - o)
        out.append((off0 + o, l))
        o += l
    return out


def cdiv(a, b):
    return (a + b - 1) // b



class Prog:
    def __init__(self, name, ntok, launch="B"):
        self.nc = bass.Bass("TRN2", target_bir_lowering=False, name=name)
        self.stack = contextlib.ExitStack()
        self.cnt = {}
        self._sems = {}
        self.sbufs = {}
        self.consts = {}
        self.ntok = ntok
        self.launch = launch
        if launch == "B":
            self.bf1 = self.sbuf("bf1", [128, DC * ntok], BF16)
            self.bf2 = self.sbuf("bf2", [128, DC * ntok], BF16)
            self.bf3 = self.sbuf("bf3", [128, 8192], BF16)
            self.f1 = self.sbuf("f1", [128, 6336], F32)
            self.f2 = None
            self.sr1 = None
        else:
            self.bf1 = self.sbuf("bf1", [128, 32768], BF16)
            self.bf2 = self.sbuf("bf2", [128, 25600], BF16)
            self.bf3 = None
            self.f1 = self.sbuf("f1", [128, 4352], F32)
            self.f2 = self.sbuf("f2", [128, 8192], F32)
            self.sr1 = self.sbuf("sr1", [128, 4160], BF16)
        self.srow = self.sbuf("srow", [1, 1536], F32)
        self.smu = self.sbuf("smu", [128, 4160], BF16)
        self.sr = self.sbuf("sr", [128, 4160], F32)
        self.vecs = self.sbuf("vecs", [128, 144], F32)
        self.P = [self.psum(f"P{i}", [128, 512]) for i in range(4)]
        self.PREP = self.psum("PREP", [128, 1024])
        self.PAUX = self.psum("PAUX", [128, 1024])
        self.vec_used = 0
        self.c_ones_bf = self.sbuf("c_ones_bf", [128, 1], BF16)
        self.c_ones_f = self.sbuf("c_ones_f", [1, 128], F32)
        self.c_onesrow_bf = self.sbuf("c_onesrow_bf", [1, 128], BF16)
        self.c_onescol_f = self.sbuf("c_onescol_f", [128, 1], F32)
        self.c_ident_f = self.sbuf("c_ident_f", [128, 128], F32)

    def emit_const_init(self):
        ob = self.const("i_ones_bf", np.ones((128, 1), dtype=ml_dtypes.bfloat16))
        of = self.const("i_ones_f", np.ones((1, 128), dtype=np.float32))
        orb = self.const("i_onesrow_bf", np.ones((1, 128), dtype=ml_dtypes.bfloat16))
        oc = self.const("i_onescol_f", np.ones((128, 1), dtype=np.float32))
        idf = self.const("i_ident_f", np.eye(128, dtype=np.float32))
        with phase(self, "const_init") as b:
            def gv(e):
                b.dma(e, "cinit", self.c_ones_bf[:, :], ob.ap())
                b.dma(e, "cinit", self.c_ones_f[:, :], of.ap())
                b.dma(e, "cinit", self.c_onesrow_bf[:, :], orb.ap())
                b.dma(e, "cinit", self.c_onescol_f[:, :], oc.ap())
                b.dma(e, "cinit", self.c_ident_f[:, :], idf.ap())
                b.w(e, "cinit")
            b.g(gv)

    PERSISTENT_SEMS = ("vself", "cinit")

    def sem(self, name):
        if name not in self._sems:
            if not hasattr(self, "_free_handles"):
                self._free_handles = []
                self._phys = {}
                self._base = {}
                self._nsem = 0
            if self._free_handles:
                h = self._free_handles.pop()
            else:
                h = self.stack.enter_context(
                    self.nc.semaphore(f"sem{self._nsem}"))
                self._nsem += 1
                self._phys[h.num] = 0
            self._sems[name] = h
            self._base[name] = self._phys[h.num]
            self.cnt[name] = 0
        return self._sems[name]

    def retire_sems(self, keep=()):
        for name in list(self._sems):
            if name in self.PERSISTENT_SEMS or name in keep:
                continue
            h = self._sems.pop(name)
            self._free_handles.append(h)
            del self.cnt[name]
            del self._base[name]

    def sbuf(self, name, shape, dt=F32):
        if name not in self.sbufs:
            self.sbufs[name] = self.stack.enter_context(
                self.nc.sbuf_tensor(name, list(shape), dt))
        return self.sbufs[name]

    def psum(self, name, shape, dt=F32):
        return self.stack.enter_context(self.nc.psum_tensor(name, list(shape), dt))

    def const(self, name, arr):
        if name not in self.consts:
            self.consts[name] = self.nc.inline_tensor(np.ascontiguousarray(arr), name)
        return self.consts[name]

    def alloc_vec(self, ncols):
        a = self.vec_used
        self.vec_used += ncols
        assert self.vec_used <= 144
        return self.vecs[:, a:a + ncols]

    def bfv(self, arena, lo, n, c):
        return arena[:, lo:lo + n].rearrange("p (c t) -> p c t", c=c)

    def ln_view(self):
        return self.bf1[:].rearrange("p (c t) -> p c t", c=DC)

    def u_view(self):
        return self.bf2[:].rearrange("p (c t) -> p c t", c=DC)

    def close(self):
        self.stack.close()


class B:
    def __init__(self, prog, blk):
        self.p = prog
        self.blk = blk

    def _run(self, eng, f):
        getattr(self.blk, eng)(f)

    def g(self, f): self._run("gpsimd", f)
    def t(self, f): self._run("tensor", f)
    def v(self, f): self._run("vector", f)
    def s(self, f): self._run("scalar", f)
    def y(self, f): self._run("sync", f)

    def dma(self, e, sem_name, dst, src, **kw):
        sem = self.p.sem(sem_name)
        self.p.cnt[sem_name] += 16
        self.p._phys[sem.num] += 16
        e.dma_start(out=dst, in_=src, **kw).then_inc(sem, 16)
        return self.p.cnt[sem_name]

    def w(self, e, sem_name, tgt=None):
        sem = self.p.sem(sem_name)
        if tgt is None:
            tgt = self.p.cnt[sem_name]
        if tgt > 0:
            e.wait_ge(sem, self.p._base[sem_name] + tgt)

    def inc(self, ins, sem_name, n=1):
        sem = self.p.sem(sem_name)
        self.p.cnt[sem_name] += n
        self.p._phys[sem.num] += n
        ins.then_inc(sem, n)
        return self.p.cnt[sem_name]

    def vsync(self, e, ins, name="vself"):
        sem = self.p.sem(name)
        self.p.cnt[name] += 1
        self.p._phys[sem.num] += 1
        ins.then_inc(sem, 1)
        e.wait_ge(sem, self.p._base[name] + self.p.cnt[name])


class PhaseCtx:
    def __init__(self, prog, name):
        self.prog = prog
        self.name = name

    def __enter__(self):
        self.blk = self.prog.nc.Block(self.name)
        self.blk.__enter__()
        return B(self.prog, self.blk)

    def __exit__(self, *a):
        return self.blk.__exit__(*a)


def phase(prog, name):
    return PhaseCtx(prog, name)



def x_ap(x_dram, off, ln):
    return x_dram[:, off:off + ln].rearrange("(c p) t -> p c t", p=128)


def emit_stats(p, tag, x_dram, ntok, with_r1=False, eps=LNEPS0):
    ones_bf = p.c_ones_bf
    ones_f = p.c_ones_f
    if p.launch == "B":
        xbf = p.bfv(p.bf3, 0, 4352, DC)
    else:
        xbf = p.bfv(p.bf2, 4352, 4352, DC)
    x2bf = p.bfv(p.bf2, 0, 4352, DC)
    tbs = tb_list(ntok, 512)
    s_ld, s_sq, s_mm, s_row, s_rep, s_cp = (f"{tag}ld", f"{tag}sq", f"{tag}mm",
                                            f"{tag}ro", f"{tag}rp", f"{tag}cp")
    with phase(p, f"{tag}_stats") as b:
        for i, (off, ln) in enumerate(tbs):
            def gbody(e, i=i, off=off, ln=ln):
                if i >= 1:
                    b.w(e, s_mm, 2 * i)
                b.dma(e, s_ld, xbf[:, :, 0:ln], x_ap(x_dram, off, ln))
            b.g(gbody)
            def sbody(e, i=i, ln=ln):
                b.w(e, s_ld)
                if i >= 1:
                    b.w(e, s_mm, 2 * i)
                ins = e.activation(x2bf[:, :, 0:ln], xbf[:, :, 0:ln], AF.Square)
                b.inc(ins, s_sq)
            b.s(sbody)
            slot = i % 2
            def tbody(e, i=i, ln=ln, slot=slot):
                b.w(e, s_sq)
                if i >= 2:
                    b.w(e, s_row, 3 * (i - 2) + 1)
                for c in range(DC):
                    e.matmul(p.PAUX[0:1, 512 * slot:512 * slot + ln], ones_bf[:, :],
                             xbf[:, c, 0:ln], start=(c == 0), stop=(c == DC - 1))
                ins = None
                for c in range(DC):
                    ins = e.matmul(p.PAUX[64:65, 512 * slot:512 * slot + ln],
                                   ones_bf[:, :], x2bf[:, c, 0:ln],
                                   start=(c == 0), stop=(c == DC - 1))
                b.inc(ins, s_mm, 2)
            b.t(tbody)
            def vrow(e, i=i, ln=ln, slot=slot):
                b.w(e, s_mm)
                R = p.srow
                ins = e.tensor_scalar(R[0:1, 0:ln], p.PAUX[0:1, 512 * slot:512 * slot + ln],
                                      1.0 / D, None, ALU.mult)
                e.tensor_scalar(R[0:1, 512:512 + ln],
                                p.PAUX[64:65, 512 * slot:512 * slot + ln],
                                1.0 / D, None, ALU.mult)
                b.vsync(e, ins)
                ins = e.tensor_tensor(R[0:1, 1024:1024 + ln], R[0:1, 0:ln], R[0:1, 0:ln],
                                      ALU.mult)
                b.vsync(e, ins)
                ins = e.tensor_tensor(R[0:1, 512:512 + ln], R[0:1, 512:512 + ln],
                                      R[0:1, 1024:1024 + ln], ALU.subtract)
                b.vsync(e, ins)
                if with_r1:
                    ins = e.tensor_scalar(R[0:1, 1024:1024 + ln], R[0:1, 512:512 + ln],
                                          1.0 + LNEPS1, LNEPS0 * LNEPS1, ALU.mult, ALU.add)
                    b.vsync(e, ins)
                ins = e.tensor_scalar(R[0:1, 512:512 + ln], R[0:1, 512:512 + ln],
                                      1.0, eps, ALU.mult, ALU.add)
                b.inc(ins, s_row)
            b.v(vrow)
            def srow2(e, i=i, ln=ln):
                b.w(e, s_row, 3 * i + 1)
                ins = e.activation(p.srow[0:1, 512:512 + ln],
                                   p.srow[0:1, 512:512 + ln], AF.Sqrt)
                if with_r1:
                    ins = e.activation(p.srow[0:1, 1024:1024 + ln],
                                       p.srow[0:1, 1024:1024 + ln], AF.Sqrt)
                b.inc(ins, s_row)
            b.s(srow2)
            def vrow2(e, i=i, ln=ln):
                b.w(e, s_row, 3 * i + 2)
                ins = e.reciprocal(p.srow[0:1, 512:512 + ln], p.srow[0:1, 512:512 + ln])
                if with_r1:
                    ins = e.reciprocal(p.srow[0:1, 1024:1024 + ln],
                                       p.srow[0:1, 1024:1024 + ln])
                b.inc(ins, s_row)
            b.v(vrow2)
            def trep(e, i=i, ln=ln, slot=slot):
                b.w(e, s_row, 3 * (i + 1))
                if i >= 2:
                    b.w(e, s_cp, i - 1)
                e.matmul(p.PREP[:, 512 * slot:512 * slot + ln], ones_f[:, :],
                         p.srow[0:1, 0:ln], start=True, stop=True)
                ins = e.matmul(p.P[2 + slot][:, 0:ln], ones_f[:, :],
                               p.srow[0:1, 512:512 + ln], start=True, stop=True)
                if with_r1:
                    ins = e.matmul(p.P[slot][:, 0:ln], ones_f[:, :],
                                   p.srow[0:1, 1024:1024 + ln], start=True, stop=True)
                b.inc(ins, s_rep)
            b.t(trep)
            def vcp(e, i=i, off=off, ln=ln, slot=slot):
                b.w(e, s_rep)
                e.tensor_copy(p.smu[:, off:off + ln],
                              p.PREP[:, 512 * slot:512 * slot + ln])
                ins = e.tensor_copy(p.sr[:, off:off + ln], p.P[2 + slot][:, 0:ln])
                if with_r1:
                    ins = e.tensor_copy(p.sr1[:, off:off + ln], p.P[slot][:, 0:ln])
                b.inc(ins, s_cp)
            b.v(vcp)


def emit_ln(p, tag, x_dram, ln_buf, tbs, r_arena=None, buf_base=0, gb=None,
            pre=None):
    if r_arena is None:
        r_arena = p.sr
    xf = p.f1[:, 0:4352].rearrange("p (c t) -> p c t", c=DC)
    s_ld, s_v = f"{tag}ld", f"{tag}v"
    with phase(p, f"{tag}_ln") as b:
        if pre is not None:
            pre(b)
        for i, (off, ln) in enumerate(tbs):
            bo = off - buf_base
            def ybody(e, i=i, off=off, ln=ln):
                if i >= 1:
                    b.w(e, s_v, i)
                b.dma(e, s_ld, xf[:, :, 0:ln], x_ap(x_dram, off, ln))
            b.y(ybody)
            def vbody(e, i=i, off=off, ln=ln, bo=bo):
                b.w(e, s_ld)
                ins = None
                for c in range(DC):
                    ins = e.tensor_tensor(xf[:, c, 0:ln], xf[:, c, 0:ln],
                                         p.smu[:, off:off + ln], ALU.subtract)
                b.vsync(e, ins)
                if gb is None:
                    for c in range(DC):
                        ins = e.tensor_tensor(ln_buf[:, c, bo:bo + ln],
                                              xf[:, c, 0:ln],
                                              r_arena[:, off:off + ln], ALU.mult)
                    b.inc(ins, s_v)
                else:
                    for c in range(DC):
                        ins = e.tensor_tensor(xf[:, c, 0:ln], xf[:, c, 0:ln],
                                              r_arena[:, off:off + ln], ALU.mult)
                    b.vsync(e, ins)
                    gm, bt = gb
                    for c in range(DC):
                        ins = e.tensor_scalar(ln_buf[:, c, bo:bo + ln],
                                              xf[:, c, 0:ln], gm[:, c:c + 1],
                                              bt[:, c:c + 1], ALU.mult, ALU.add)
                    b.inc(ins, s_v)
            b.v(vbody)



def emit_s5(p, xin, xout, W, stop_after=None):
    ntok = p.ntok
    ln = p.ln_view()
    u = p.u_view()
    out_buf = ln
    acc = p.f1[:, 0:DC * SB_B].rearrange("p (c t) -> p c t", c=DC)
    wslab = p.bfv(p.bf3, 0, 8192, DC)
    tbs = tb_list(ntok, 512)

    emit_stats(p, "s5a", xin, ntok)
    emit_ln(p, "s5b", xin, ln, tbs)

    b_in = W["b_in"]; b_out = W["b_out"]; gam = W["gamma"]

    with phase(p, "s5_win") as b:
        def gbody(e):
            for k in range(DC):
                b.dma(e, "s5w", wslab[:, k, :], W["w_inT"][128 * k:128 * (k + 1), :])
        b.g(gbody)
        def sbody(e):
            ins = e.activation(gam, W["gamma_raw"], AF.Sigmoid)
            b.inc(ins, "s5g")
        b.s(sbody)
        def vbody(e):
            b.w(e, "s5g")
            ins = e.tensor_scalar(W["one_m_g"], gam, -1.0, 1.0, ALU.mult, ALU.add)
            b.inc(ins, "s5g")
        b.v(vbody)
        nt = 0
        for m in range(DC):
            for it, (off, lnn) in enumerate(tbs):
                i = nt; nt += 1
                bank = i % 4
                def tbody(e, m=m, off=off, lnn=lnn, bank=bank, i=i):
                    if i == 0:
                        b.w(e, "s5w")
                    if i >= 4:
                        b.w(e, "s5ev", i - 3)
                    ins = None
                    for k in range(DC):
                        ins = e.matmul(p.P[bank][:, 0:lnn],
                                       wslab[:, k, 128 * m:128 * (m + 1)],
                                       ln[:, k, off:off + lnn],
                                       start=(k == 0), stop=(k == DC - 1))
                    b.inc(ins, "s5mm")
                b.t(tbody)
                def sbody2(e, m=m, off=off, lnn=lnn, bank=bank, i=i):
                    b.w(e, "s5mm", i + 1)
                    ins = e.activation(u[:, m, off:off + lnn], p.P[bank][:, 0:lnn],
                                       AF.Silu, bias=b_in[:, m:m + 1], scale=1.0)
                    b.inc(ins, "s5ev")
                b.s(sbody2)

    if stop_after == "win":
        return
    with phase(p, "s5_ema") as b:
        def vbody(e):
            for t in range(NMB):
                o = t * SB_B
                ins = None
                for c in range(DC):
                    if t == 0:
                        ins = e.tensor_scalar(acc[:, c, :], u[:, c, o:o + SB_B], 1.0,
                                              None, ALU.mult)
                    else:
                        ins = e.scalar_tensor_tensor(acc[:, c, :], acc[:, c, :],
                                                     W["one_m_g"][:, c:c + 1],
                                                     u[:, c, o:o + SB_B], ALU.mult, ALU.add)
                b.vsync(e, ins)
                for c in range(DC):
                    e.tensor_copy(out_buf[:, c, o:o + SB_B], acc[:, c, :])
        b.v(vbody)

    if stop_after == "ema":
        return
    xf = p.f1[:, 0:4352].rearrange("p (c t) -> p c t", c=DC)
    tnh = p.f1[:, 4352:5376].rearrange("p (b t) -> p b t", b=2)
    with phase(p, "s5_wout") as b:
        def gbody(e):
            for k in range(DC):
                b.dma(e, "s5w2", wslab[:, k, :], W["w_outT"][128 * k:128 * (k + 1), :])
        b.g(gbody)
        def vbody(e):
            b.w(e, "s5w2")
            ins = None
            for k in range(DC):
                ins = e.tensor_scalar(wslab[:, k, :], wslab[:, k, :],
                                      gam[:, k:k + 1], None, ALU.mult)
            b.inc(ins, "s5sc")
        b.v(vbody)
        nt = 0
        for it, (off, lnn) in enumerate(tbs):
            def ybody(e, off=off, lnn=lnn, it=it):
                if it >= 1:
                    b.w(e, "s5st")
                b.dma(e, "s5x", xf[:, :, 0:lnn], x_ap(xin, off, lnn))
            b.y(ybody)
            for m in range(DC):
                i = nt; nt += 1
                bank = i % 4
                def tbody(e, m=m, off=off, lnn=lnn, bank=bank, i=i):
                    if i == 0:
                        b.w(e, "s5sc")
                    if i >= 4:
                        b.w(e, "s5ev2", i - 3)
                    ins = None
                    for k in range(DC):
                        ins = e.matmul(p.P[bank][:, 0:lnn],
                                       wslab[:, k, 128 * m:128 * (m + 1)],
                                       out_buf[:, k, off:off + lnn],
                                       start=(k == 0), stop=(k == DC - 1))
                    b.inc(ins, "s5mm2")
                b.t(tbody)
                def sbody(e, m=m, off=off, lnn=lnn, bank=bank, i=i):
                    b.w(e, "s5mm2", i + 1)
                    if i >= 2:
                        b.w(e, "s5ev2", i - 1)
                    ins = e.activation(tnh[:, i % 2, 0:lnn], p.P[bank][:, 0:lnn],
                                       AF.Tanh, bias=b_out[:, m:m + 1], scale=1.0)
                    b.inc(ins, "s5th")
                b.s(sbody)
                def vbody2(e, m=m, off=off, lnn=lnn, bank=bank, i=i, it=it):
                    b.w(e, "s5th", i + 1)
                    b.w(e, "s5x", 16 * (it + 1))
                    ins = e.tensor_tensor(xf[:, m, 0:lnn], xf[:, m, 0:lnn],
                                          tnh[:, i % 2, 0:lnn], ALU.add)
                    b.inc(ins, "s5ev2")
                b.v(vbody2)
            def ybody2(e, off=off, lnn=lnn, it=it):
                b.w(e, "s5ev2", (it + 1) * DC)
                b.dma(e, "s5st", x_ap(xout, off, lnn), xf[:, :, 0:lnn])
            b.y(ybody2)
        def yfin(e):
            b.w(e, "s5st")
        b.y(yfin)



def emit_s6(p, xin, xout, W, tag):
    ntok = p.ntok
    ln = p.ln_view()
    ones_f = p.c_ones_f
    onescol_f = p.c_onescol_f
    ident_f = p.c_ident_f
    tbs = tb_list(ntok, 512)

    emit_stats(p, tag + "a", xin, ntok)
    emit_ln(p, tag + "b", xin, ln, tbs)

    logits = p.f1[0:16, 0:4144]
    sel = p.sbuf("s6_sel", [128, DC, 16], F32)
    sSel = p.sbuf("s6_ssel", [16, 1], F32)
    xr = p.f1[:, 4352:6336].rearrange("p (c t) -> p c t", c=DC)
    rtb = tb_list(ntok, 248)
    s_ld, s_mm, s_v = tag + "rl", tag + "rm", tag + "rv"
    with phase(p, tag + "_router") as b:
        def ybody(e):
            b.dma(e, s_ld, sel[:, :, :],
                  W["sel_T"].rearrange("(c p) e -> p c e", p=128))
        b.y(ybody)
        def t0(e):
            b.w(e, s_ld)
            ins = None
            for k in range(DC):
                ins = e.matmul(p.PAUX[0:16, 0:1], sel[:, k, :], onescol_f[:, :],
                               start=(k == 0), stop=(k == DC - 1))
            b.inc(ins, s_mm)
        b.t(t0)
        def v1(e):
            b.w(e, s_mm)
            e.tensor_copy(sSel[:, :], p.PAUX[0:16, 0:1])
        b.v(v1)
        for i, (off, lnn) in enumerate(rtb):
            slot = i % 2
            def ybody2(e, i=i, off=off, lnn=lnn):
                if i >= 1:
                    b.w(e, s_v, i)
                b.dma(e, tag + "rx", xr[:, :, 0:lnn], x_ap(xin, off, lnn))
            b.y(ybody2)
            def tbody(e, i=i, off=off, lnn=lnn, slot=slot):
                if i >= 2:
                    b.w(e, s_v, i - 1)
                b.w(e, tag + "rx")
                ins = None
                for k in range(DC):
                    ins = e.matmul(p.PAUX[0:16, 512 * slot:512 * slot + lnn],
                                   sel[:, k, :], xr[:, k, 0:lnn],
                                   start=(k == 0), stop=(k == DC - 1))
                b.inc(ins, s_mm)
            b.t(tbody)
            def vbody(e, i=i, off=off, lnn=lnn, slot=slot):
                b.w(e, s_mm)
                tmp = logits[:, off:off + lnn]
                ins = e.tensor_scalar(tmp, p.smu[0:16, off:off + lnn], sSel[:, 0:1],
                                      None, ALU.mult)
                b.vsync(e, ins)
                ins = e.tensor_tensor(tmp, p.PAUX[0:16, 512 * slot:512 * slot + lnn],
                                      tmp, ALU.subtract)
                b.vsync(e, ins)
                ins = e.tensor_tensor(tmp, tmp, p.sr[0:16, off:off + lnn], ALU.mult)
                b.inc(ins, s_v)
            b.v(vbody)

    c_fm = p.bf2[0:16, 0:4144]
    TM = p.f1[:, 4352:4448]
    rs_b = W["rs_b"]
    with phase(p, tag + "_top4") as b:
        def y0(e):
            b.dma(e, tag + "t0", p.srow[0:1, 0:1], W["route_scale"])
        b.y(y0)
        def t1(e):
            b.w(e, tag + "t0")
            ins = e.matmul(p.PREP[:, 0:1], ones_f[:, :], p.srow[0:1, 0:1],
                           start=True, stop=True)
            b.inc(ins, tag + "t1")
        b.t(t1)
        def v1b(e):
            b.w(e, tag + "t1")
            e.tensor_copy(rs_b, p.PREP[:, 0:1])
        b.v(v1b)
        ntile = cdiv(ntok, 128)
        s_tr, s_w = tag + "tr", tag + "tw"
        for i in range(ntile):
            o = 128 * i
            w_ = min(128, ntok - o)
            pslot = i % 2
            def tbody(e, i=i, o=o, w_=w_, pslot=pslot):
                if i >= 2:
                    b.w(e, s_w, i - 1)
                ins = e.transpose(p.PAUX[0:w_, 512 * pslot:512 * pslot + 16],
                                  logits[:, o:o + w_], ident_f[0:16, 0:16])
                b.inc(ins, s_tr)
            b.t(tbody)
            def vsbody(e, i=i, o=o, w_=w_, pslot=pslot):
                b.w(e, s_tr, i + 1)
                if i >= 1:
                    b.w(e, tag + "tc", i)
                lt = TM[0:w_, 0:16]
                wk = TM[0:w_, 16:32]
                mx = TM[0:w_, 32:33]
                msk = TM[0:w_, 48:64]
                ins = e.tensor_copy(lt, p.PAUX[0:w_, 512 * pslot:512 * pslot + 16])
                b.vsync(e, ins)
                ins = e.tensor_copy(wk, lt)
                b.vsync(e, ins)
                for r in range(3):
                    ins = e.reduce_max(mx, wk, AX.X)
                    b.vsync(e, ins)
                    ins = e.tensor_scalar(msk, wk, mx, None, ALU.is_ge)
                    b.vsync(e, ins)
                    ins = e.scalar_tensor_tensor(wk, msk, BIG_NEG, wk, ALU.mult, ALU.add)
                    b.vsync(e, ins)
                ins = e.reduce_max(mx, wk, AX.X)
                b.vsync(e, ins)
                ins = e.tensor_scalar(msk, lt, mx, None, ALU.is_ge)
                b.inc(ins, tag + "tm")
            b.v(vsbody)
            def sbody(e, i=i, w_=w_):
                b.w(e, tag + "tm", i + 1)
                if i >= 1:
                    b.w(e, tag + "tt", i)
                lt = TM[0:w_, 0:16]
                pp = TM[0:w_, 64:80]
                ins = e.activation(pp, lt, AF.Sigmoid)
                b.inc(ins, tag + "tp")
            b.s(sbody)
            def v2body(e, i=i, o=o, w_=w_, pslot=pslot):
                b.w(e, tag + "tp", i + 1)
                pp = TM[0:w_, 64:80]
                msk = TM[0:w_, 48:64]
                sm = TM[0:w_, 33:34]
                ins = e.tensor_tensor(pp, pp, msk, ALU.mult)
                b.vsync(e, ins)
                ins = e.tensor_reduce(sm, pp, AX.X, ALU.add)
                b.vsync(e, ins)
                ins = e.tensor_scalar(sm, sm, 1e-9, None, ALU.max)
                b.vsync(e, ins)
                ins = e.reciprocal(sm, sm)
                b.vsync(e, ins)
                ins = e.tensor_tensor(sm, sm, rs_b[0:w_, :], ALU.mult)
                b.vsync(e, ins)
                ins = e.tensor_scalar(pp, pp, sm, None, ALU.mult)
                b.inc(ins, tag + "tc")
            b.v(v2body)
            def tbody2(e, i=i, o=o, w_=w_, pslot=pslot):
                b.w(e, tag + "tc", i + 1)
                ins = e.transpose(p.PAUX[0:16, 512 * pslot + 16:512 * pslot + 16 + w_],
                                  TM[0:w_, 64:80], ident_f[0:w_, 0:w_])
                b.inc(ins, tag + "tt")
            b.t(tbody2)
            def v3body(e, i=i, o=o, w_=w_, pslot=pslot):
                b.w(e, tag + "tt", i + 1)
                ins = e.tensor_copy(c_fm[:, o:o + w_],
                                    p.PAUX[0:16, 512 * pslot + 16:512 * pslot + 16 + w_])
                b.inc(ins, s_w)
            b.v(v3body)
        def ydump(e):
            b.w(e, s_w)
            b.dma(e, tag + "cd", W["c_dram"][0:16, 0:ntok], c_fm[:, 0:ntok])
            b.w(e, tag + "cd")
        b.y(ydump)

    NQ = 8
    EG = 2
    NGRP = 16 // EG
    qlen = cdiv(ntok, NQ)
    y_q = p.f1[:, 0:4144].rearrange("p (c t) -> p c t", c=DC)
    kg = p.bf3[:, 0:4096].rearrange("p (g c t) -> p g c t", g=EG, c=DC)
    vg = p.bf3[:, 4096:8192].rearrange("p (g c t) -> p g c t", g=EG, c=2)
    sc_grp = p.bf2[:, 8288:8288 + EG * 2 * 520].rearrange(
        "p (g c t) -> p g c t", g=EG, c=2)
    crep = p.bf2[:, 10368:10368 + EG * 520].rearrange(
        "p (g t) -> p g t", g=EG)
    cd = W["c_dram"]
    xr2 = p.f1[:, 4144:6192].rearrange("p (c t) -> p c t", c=DC)

    for q in range(NQ):
        q_off = q * qlen
        q_ln = min(qlen, ntok - q_off)
        qtbs = tb_list(q_ln, 512)
        with phase(p, f"{tag}_q{q}") as b:
            def vz(e):
                ins = e.memset(y_q[:, :, 0:q_ln], 0.0)
                b.vsync(e, ins)
            b.v(vz)
            for g in range(NGRP):
                s_kv = f"{tag}kv"
                def gld(e, g=g, s_kv=s_kv, q_off=q_off, q_ln=q_ln):
                    b.w(e, f"{tag}smm")
                    b.w(e, f"{tag}sev")
                    b.w(e, f"{tag}ymm")
                    for ei in range(EG):
                        ee = g * EG + ei
                        b.dma(e, s_kv, kg[:, ei, :, :],
                              W["keys"][ee].rearrange("(c p) h -> p c h", p=128))
                        b.dma(e, s_kv, vg[:, ei, :, :],
                              W["values"][ee].rearrange("(c p) v -> p c v", p=128))
                        csrc = bass.AP(cd, ee * 4160 + q_off, [[0, 128], [1, q_ln]])
                        b.dma(e, s_kv, crep[:, ei, 0:q_ln], csrc)
                b.g(gld)
                smm_b = p.cnt.get(f"{tag}smm", 0)
                srl_b = p.cnt.get(f"{tag}srl", 0)
                sev_b = p.cnt.get(f"{tag}sev", 0)
                yev_full = p.cnt.get(f"{tag}yev", 0)
                ymm_full = p.cnt.get(f"{tag}ymm", 0)
                nt = 0
                for ei in range(EG):
                    ee = g * EG + ei
                    for m in range(2):
                        for it, (off_, lnn) in enumerate(qtbs):
                            off = q_off + off_
                            i = nt; nt += 1
                            bank = i % 4
                            def tb1(e, ei=ei, ee=ee, m=m, off=off, off_=off_,
                                    lnn=lnn, bank=bank, i=i, s_kv=s_kv, smm_b=smm_b,
                                    srl_b=srl_b, sev_b=sev_b, yev_full=yev_full):
                                if i == 0:
                                    b.w(e, s_kv)
                                    b.w(e, f"{tag}yev", yev_full)
                                if i >= 4:
                                    b.w(e, f"{tag}srl", srl_b + i - 3)
                                if i >= 2:
                                    b.w(e, f"{tag}sev", sev_b + i - 1)
                                ins = None
                                for k in range(DC):
                                    ins = e.matmul(p.P[bank][:, 0:lnn],
                                                   kg[:, ei, k, 128 * m:128 * (m + 1)],
                                                   ln[:, k, off:off + lnn],
                                                   start=(k == 0), stop=(k == DC - 1))
                                b.inc(ins, f"{tag}smm")
                            b.t(tb1)
                            def sb1(e, ei=ei, m=m, off_=off_, lnn=lnn, bank=bank,
                                    i=i, smm_b=smm_b, ymm_full=ymm_full):
                                b.w(e, f"{tag}smm", smm_b + (i + 1))
                                if i == 0:
                                    b.w(e, f"{tag}ymm", ymm_full)
                                ins = e.activation(sc_grp[:, ei, m, off_:off_ + lnn],
                                                   p.P[bank][:, 0:lnn], AF.Relu)
                                b.inc(ins, f"{tag}srl")
                            b.s(sb1)
                            def vb1(e, ei=ei, m=m, off_=off_, lnn=lnn, bank=bank,
                                    i=i, srl_b=srl_b):
                                b.w(e, f"{tag}srl", srl_b + i + 1)
                                sc = sc_grp[:, ei, m, off_:off_ + lnn]
                                ins = e.tensor_tensor(
                                    sc, sc, crep[:, ei, off_:off_ + lnn], ALU.mult)
                                b.inc(ins, f"{tag}sev")
                            b.v(vb1)
                ymm_b = p.cnt.get(f"{tag}ymm", 0)
                yev_b = p.cnt.get(f"{tag}yev", 0)
                nt2 = 0
                for m in range(DC):
                    for it, (off_, lnn) in enumerate(qtbs):
                        i = nt2; nt2 += 1
                        bank = i % 4
                        def tb2(e, g=g, m=m, off_=off_, lnn=lnn, bank=bank, i=i,
                                yev_b=yev_b):
                            if i == 0:
                                b.w(e, f"{tag}sev")
                            if i >= 4:
                                b.w(e, f"{tag}yev", yev_b + i - 3)
                            first = True
                            ins = None
                            for ei in range(EG):
                                for k in range(2):
                                    ins = e.matmul(p.P[bank][:, 0:lnn],
                                                   vg[:, ei, k, 128 * m:128 * (m + 1)],
                                                   sc_grp[:, ei, k, off_:off_ + lnn],
                                                   start=first,
                                                   stop=(ei == EG - 1 and k == 1))
                                    first = False
                            b.inc(ins, f"{tag}ymm")
                        b.t(tb2)
                        def vb2(e, m=m, off_=off_, lnn=lnn, bank=bank, i=i,
                                ymm_b=ymm_b, yev_b=yev_b):
                            b.w(e, f"{tag}ymm", ymm_b + i + 1)
                            if i == 0:
                                b.w(e, f"{tag}yev", yev_b)
                            ins = e.tensor_tensor(y_q[:, m, off_:off_ + lnn],
                                                  y_q[:, m, off_:off_ + lnn],
                                                  p.P[bank][:, 0:lnn], ALU.add)
                            b.inc(ins, f"{tag}yev")
                        b.v(vb2)
            for it, (off_, lnn) in enumerate(tb_list(q_ln, 256)):
                off = q_off + off_
                def yb(e, it=it, off=off, lnn=lnn):
                    if it >= 1:
                        b.w(e, f"{tag}xst")
                    b.dma(e, f"{tag}xl", xr2[:, :, 0:lnn], x_ap(xin, off, lnn))
                b.y(yb)
                def vb3(e, it=it, off=off, off_=off_, lnn=lnn):
                    b.w(e, f"{tag}xl")
                    b.w(e, f"{tag}yev")
                    ins = None
                    for c in range(DC):
                        ins = e.tensor_tensor(xr2[:, c, 0:lnn], xr2[:, c, 0:lnn],
                                              y_q[:, c, off_:off_ + lnn], ALU.add)
                    b.inc(ins, f"{tag}xv")
                b.v(vb3)
                def yb2(e, it=it, off=off, lnn=lnn):
                    b.w(e, f"{tag}xv")
                    b.dma(e, f"{tag}xst", x_ap(xout, off, lnn), xr2[:, :, 0:lnn])
                b.y(yb2)
            def yfin(e):
                b.w(e, f"{tag}xst")
            b.y(yfin)



def host_vec(a):
    a = np.asarray(a, dtype=np.float32).reshape(-1)
    k = a.shape[0] // 128
    return np.ascontiguousarray(a.reshape(k, 128).T)


def build_launch_B(steps):
    p = Prog("launchB_" + "".join(steps), NT_B, launch="B")
    nc = p.nc
    p.emit_const_init()
    xio_in = nc.dram_tensor("x", [D, NT_B], F32, kind="ExternalInput").ap()
    xio_out = nc.dram_tensor("xo", [D, NT_B], F32, kind="ExternalOutput").ap()
    n_steps = len(steps)
    inter = [nc.dram_tensor(f"xmid{i}", [D, NT_B], F32).ap()
             for i in range(n_steps - 1)]
    chain = [xio_in] + inter + [xio_out]

    wdecl = {}

    def dram(name, shape):
        if name not in wdecl:
            wdecl[name] = nc.dram_tensor(name, list(shape), F32,
                                         kind="ExternalInput").ap()
        return wdecl[name]

    rs_b = None
    for si, st in enumerate(steps):
        p.retire_sems()
        xin, xout = chain[si], chain[si + 1]
        if st == "s5":
            W = {
                "w_inT": dram("s5_w_inT", [D, D]),
                "w_outT": dram("s5_w_outT", [D, D]),
            }
            bin_d = dram("s5_b_in", [128, DC])
            bout_d = dram("s5_b_out", [128, DC])
            gw_d = dram("s5_gamma_raw", [128, DC])
            W["b_in"] = p.alloc_vec(DC)
            W["b_out"] = p.alloc_vec(DC)
            W["gamma_raw"] = p.alloc_vec(DC)
            W["gamma"] = p.alloc_vec(DC)
            W["one_m_g"] = p.alloc_vec(DC)
            with phase(p, "s5_vecs") as b:
                def gv(e):
                    b.dma(e, "s5vl", W["b_in"], bin_d)
                    b.dma(e, "s5vl", W["b_out"], bout_d)
                    b.dma(e, "s5vl", W["gamma_raw"], gw_d)
                    b.w(e, "s5vl")
                b.g(gv)
            emit_s5(p, xin, xout, W)
        elif st in ("s6", "s8"):
            if rs_b is None:
                rs_b = p.alloc_vec(1)
            W = {
                "sel_T": dram("l1f_sel_T", [D, 16]),
                "keys": dram("l1f_keys", [16, D, 256]),
                "values": dram("l1f_values", [16, 256, D]),
                "route_scale": dram("l1f_route_scale", [1, 1]),
                "rs_b": rs_b,
                "c_dram": nc.dram_tensor(st + "_c", [16, 4160], BF16),
            }
            emit_s6(p, xin, xout, W, tag=st)
        elif st == "s7":
            W = {
                "w_inT": dram("s7_w_inT", [D, 4096]),
                "xprojT": dram("s7_xprojT", [2048, 2080]),
                "w_outT": dram("s7_w_outT", [2048, D]),
                "conv_w": dram("s7_conv_w", [128, 16, 4]),
                "conv_b": dram("s7_conv_b", [128, 16]),
                "Dvec": dram("s7_D", [128, 16]),
                "halo_mask": dram("s7_halo_mask", [128, NMB * HALO]),
            }
            emit_s7(p, xin, xout, W)
        else:
            raise ValueError(st)
    p.close()
    return p


def prep_inputs_B(inputs, x4, steps=("s5", "s6", "s7", "s8")):
    W = inputs
    shared = {}
    if "s5" in steps:
        shared.update({
            "s5_w_inT": np.ascontiguousarray(W["L1_attn_a_W_in"].T),
            "s5_w_outT": np.ascontiguousarray(W["L1_attn_a_W_out"].T),
            "s5_b_in": host_vec(W["L1_attn_a_b_in"]),
            "s5_b_out": host_vec(W["L1_attn_a_b_out"]),
            "s5_gamma_raw": host_vec(W["L1_attn_a_weight"]),
        })
    if "s6" in steps or "s8" in steps:
        shared.update({
            "l1f_sel_T": np.ascontiguousarray(W["L1_ffn_expert_sel"].T),
            "l1f_keys": np.ascontiguousarray(W["L1_ffn_keys"]),
            "l1f_values": np.ascontiguousarray(W["L1_ffn_values"]),
            "l1f_route_scale": np.asarray(W["L1_ffn_route_scale"],
                                          np.float32).reshape(1, 1),
        })
    if "s7" in steps:
        shared.update({
            "s7_w_inT": np.ascontiguousarray(W["L1_attn_b_W_in"].T),
            "s7_xprojT": np.ascontiguousarray(W["L1_attn_b_W_xproj"].T),
            "s7_w_outT": np.ascontiguousarray(W["L1_attn_b_W_out"].T),
            "s7_conv_w": np.ascontiguousarray(
                W["L1_attn_b_conv_w"].reshape(16, 128, 4).transpose(1, 0, 2)),
            "s7_conv_b": host_vec(W["L1_attn_b_conv_b"]),
            "s7_D": host_vec(W["L1_attn_b_D"]),
        })
    in_maps = []
    for c in range(NCORES):
        xs = np.zeros((NMB, SB_B, D), dtype=np.float32)
        lo = 256 * c - HALO
        if c == 0:
            xs[:, HALO:, :] = x4[:, 0:256, :]
        else:
            xs[:, :, :] = x4[:, lo:lo + SB_B, :]
        xfm = np.ascontiguousarray(xs.reshape(NMB * SB_B, D).T)
        m = {"x": xfm}
        m.update(shared)
        if "s7" in steps:
            hm = np.ones((128, NMB * HALO), np.float32)
            if c == 0:
                hm[:] = 0.0
            m["s7_halo_mask"] = hm
        in_maps.append(m)
    return in_maps


def finish_outputs_B(results):
    xf = np.empty((NMB, SEQ, D), dtype=np.float32)
    for c, r in enumerate(results):
        xo = np.asarray(r["xo"])
        t = xo.reshape(D, NMB, SB_B).transpose(1, 2, 0)
        xf[:, 256 * c:256 * (c + 1), :] = t[:, HALO:, :]
    return xf



def emit_s7(p, xin, xout, W):
    nc = p.nc
    ntok = p.ntok
    DI = 2048
    DIC = DI // 128
    ones_f = p.c_ones_f

    xc_dram = nc.dram_tensor("s7_xc", [DI, 4096], BF16).ap()
    sz_dram = nc.dram_tensor("s7_sz", [DI, 4096], BF16).ap()

    emit_stats(p, "s7a", xin, ntok)

    cw = p.sbuf("s7_cw", [128, 16, 4], F32)
    cb = p.sbuf("s7_cb", [128, 16], F32)
    Dv = p.sbuf("s7_dv", [128, 16], F32)
    hm = p.sbuf("s7_hm", [128, NMB * HALO], F32)
    with phase(p, "s7_vecs") as b:
        def gv(e):
            b.dma(e, "s7vl", cw[:, :, :], W["conv_w"])
            b.dma(e, "s7vl", cb[:, :], W["conv_b"])
            b.dma(e, "s7vl", Dv[:, :], W["Dvec"])
            b.dma(e, "s7vl", hm[:, :], W["halo_mask"])
            b.w(e, "s7vl")
        b.g(gv)

    w_in = p.bfv(p.bf1, 0, DC * 4096, DC)
    with phase(p, "s7_wld") as b:
        def gl(e):
            for k in range(DC):
                b.dma(e, "s7wl", w_in[:, k, :], W["w_inT"][128 * k:128 * (k + 1), :])
            b.w(e, "s7wl")
        b.g(gl)

    ln_mb = p.bfv(p.bf2, 0, DC * SB_B, DC)
    xb_mb = p.bfv(p.bf2, 2072, DIC * SB_B, DIC)
    z_mb = p.bfv(p.bf2, 2072 + 4144, DIC * SB_B, DIC)
    xcb_mb = p.bfv(p.bf2, 2072 + 2 * 4144, DIC * 256, DIC)
    szb_mb = p.bfv(p.bf2, 2072 + 2 * 4144 + 4096, DIC * 256, DIC)
    xc_f = p.f1[:, 0:4096].rearrange("p (c t) -> p c t", c=DIC)

    for mb in range(NMB):
        base = mb * SB_B
        emit_ln(p, f"s7l{mb}", xin, ln_mb, [(base, SB_B)], buf_base=base)
        with phase(p, f"s7_p1_{mb}") as b:
            for m in range(32):
                bank = m % 4
                def tb(e, m=m, bank=bank):
                    if m >= 4:
                        b.w(e, "s7ev", m - 3)
                    ins = None
                    for k in range(DC):
                        ins = e.matmul(p.P[bank][:, 0:SB_B],
                                       w_in[:, k, 128 * m:128 * (m + 1)],
                                       ln_mb[:, k, :], start=(k == 0), stop=(k == DC - 1))
                    b.inc(ins, "s7mm")
                b.t(tb)
                def sb(e, m=m, bank=bank):
                    b.w(e, "s7mm", m + 1)
                    dst = xb_mb[:, m, :] if m < 16 else z_mb[:, m - 16, :]
                    ins = e.activation(dst, p.P[bank][:, 0:SB_B], AF.Copy)
                    b.inc(ins, "s7ev")
                b.s(sb)
            def vmask(e, mb=mb):
                b.w(e, "s7ev", 16)
                ins = None
                for c in range(DIC):
                    ins = e.tensor_tensor(xb_mb[:, c, 0:HALO], xb_mb[:, c, 0:HALO],
                                          hm[:, mb * HALO:(mb + 1) * HALO], ALU.mult)
                b.vsync(e, ins)
                for c in range(DIC):
                    ins = e.tensor_scalar(xc_f[:, c, :], xb_mb[:, c, 0:256],
                                          cw[:, c, 0:1], cb[:, c:c + 1],
                                          ALU.mult, ALU.add)
                b.vsync(e, ins)
                for k in range(1, 4):
                    for c in range(DIC):
                        ins = e.scalar_tensor_tensor(xc_f[:, c, :],
                                                     xb_mb[:, c, k:k + 256],
                                                     cw[:, c, k:k + 1],
                                                     xc_f[:, c, :], ALU.mult, ALU.add)
                    b.vsync(e, ins)
                b.inc(e.engine_nop(), "s7cv")
            b.v(vmask)
            def ssl(e, mb=mb):
                b.w(e, "s7cv", 1)
                ins = e.activation(xcb_mb[:, :, :], xc_f[:, :, :], AF.Silu)
                b.vsync(e, ins, "sself")
                ins = e.activation(szb_mb[:, :, :], z_mb[:, :, HALO:SB_B], AF.Silu)
                b.inc(ins, "s7sl")
            b.s(ssl)
            def yst(e, mb=mb):
                b.w(e, "s7sl", 1)
                b.dma(e, "s7bd",
                      xc_dram[:, 256 * mb:256 * (mb + 1)].rearrange(
                          "(c p) t -> p c t", p=128), xcb_mb[:, :, :])
                b.dma(e, "s7bd",
                      sz_dram[:, 256 * mb:256 * (mb + 1)].rearrange(
                          "(c p) t -> p c t", p=128), szb_mb[:, :, :])
                b.w(e, "s7bd")
            b.y(yst)
        p.retire_sems()

    with phase(p, "s7_halo") as b:
        def yh(e):
            src = bass.AP(xin.tensor, 0, [[NT_B, D], [SB_B, NMB], [1, HALO]])
            dst = bass.AP(xout.tensor, 0, [[NT_B, D], [SB_B, NMB], [1, HALO]])
            b.dma(e, "s7hc", dst, src)
            b.w(e, "s7hc")
        b.y(yh)

    xpd = p.bfv(p.bf1, 0, DIC * 2048, DIC)
    xpbc = p.sbuf("s7_xpbc", [128, 16, 32], BF16)
    with phase(p, "s7_wld2") as b:
        def gl(e):
            for k in range(DIC):
                b.dma(e, "s7w2", xpd[:, k, :],
                      W["xprojT"][128 * k:128 * (k + 1), 32:2080])
                b.dma(e, "s7w2", xpbc[:, k, :],
                      W["xprojT"][128 * k:128 * (k + 1), 0:32])
            b.w(e, "s7w2")
        b.g(gl)

    xcs = p.bfv(p.bf2, 0, DIC * 512, DIC)
    szs = p.bfv(p.bf2, DIC * 512, DIC * 512, DIC)
    dlt = p.bfv(p.bf2, 2 * DIC * 512, DIC * 512, DIC)
    ys = p.bfv(p.bf2, 3 * DIC * 512, DIC * 512, DIC)
    wo_slab = p.bfv(p.bf3, 0, 2 * 1024, 2)
    bc_rep = p.bfv(p.bf3, 2048, 512, 1)
    xres = p.f1[:, 0:4096].rearrange("p (c t) -> p c t", c=DC)
    sp1 = p.f1[:, 4096:4608]
    sp2 = p.f1[:, 4608:5120]
    sp3 = p.f1[:, 5120:5632]
    tmv = p.f1[:, 6144:6184]

    for sp in range(8):
        o512 = sp * 512
        with phase(p, f"s7_p2_{sp}") as b:
            def gld(e, sp=sp, o512=o512):
                b.dma(e, "s7sp", xcs[:, :, :],
                      xc_dram[:, o512:o512 + 512].rearrange("(c p) t -> p c t", p=128))
                b.dma(e, "s7sp", szs[:, :, :],
                      sz_dram[:, o512:o512 + 512].rearrange("(c p) t -> p c t", p=128))
            b.g(gld)
            for m in range(DIC):
                bank = m % 4
                def tb(e, m=m, bank=bank):
                    if m == 0:
                        b.w(e, "s7sp")
                    if m >= 4:
                        b.w(e, "s7dse", m - 3)
                    ins = None
                    for k in range(DIC):
                        ins = e.matmul(p.P[bank][:, 0:512], xpd[:, k, 128 * m:128 * (m + 1)],
                                       xcs[:, k, :], start=(k == 0), stop=(k == DIC - 1))
                    b.inc(ins, "s7dmm")
                b.t(tb)
                def sb(e, m=m, bank=bank):
                    b.w(e, "s7dmm", m + 1)
                    if m >= 1:
                        b.w(e, "s7dse", m)
                    ins = e.activation(sp1, p.P[bank][:, 0:512], AF.Abs)
                    b.vsync(e, ins, "sself")
                    ins = e.activation(sp2, sp1, AF.Exp, scale=-1.0)
                    b.vsync(e, ins, "sself")
                    ins = e.activation(sp2, sp2, AF.Ln, bias=p.c_onescol_f[:, 0:1])
                    b.inc(ins, "s7dln")
                b.s(sb)
                def vb(e, m=m, bank=bank):
                    b.w(e, "s7dln", m + 1)
                    ins = e.tensor_scalar(sp3, p.P[bank][:, 0:512], 0.0, None, ALU.max)
                    b.vsync(e, ins)
                    ins = e.tensor_tensor(dlt[:, m, :], sp2, sp3, ALU.add)
                    b.inc(ins, "s7dse")
                b.v(vb)
            bsb = p.f1[0:32, 5632:6144]
            def tbc(e, sp=sp):
                b.w(e, "s7dse")
                ins = None
                for k in range(DIC):
                    ins = e.matmul(p.PAUX[0:32, 0:512], xpbc[:, k, :], xcs[:, k, :],
                                   start=(k == 0), stop=(k == DIC - 1))
                b.inc(ins, "s7bc")
            b.t(tbc)
            def vbc0(e):
                b.w(e, "s7bc")
                ins = e.tensor_copy(bsb, p.PAUX[0:32, 0:512])
                b.inc(ins, "s7be")
            b.v(vbc0)
            for gidx in range(4):
                go = 128 * gidx
                t1 = p.cnt.get("s7be", 0)
                def tg1(e, gidx=gidx, go=go, t1=t1):
                    b.w(e, "s7be", t1)
                    ins = e.transpose(p.PREP[0:128, 0:32], bsb[:, go:go + 128],
                                      p.c_ident_f[0:32, 0:32])
                    b.inc(ins, "s7bt")
                b.t(tg1)
                t2 = p.cnt.get("s7bt", 0)
                def vg1(e, gidx=gidx, go=go, t2=t2):
                    b.w(e, "s7bt", t2)
                    tm = tmv[0:128, 0:32]
                    ins = e.tensor_copy(tm, p.PREP[0:128, 0:32])
                    b.vsync(e, ins)
                    ins = e.tensor_tensor(tm[:, 0:16], tm[:, 0:16], tm[:, 16:32],
                                          ALU.mult)
                    b.vsync(e, ins)
                    ins = e.tensor_reduce(tmv[0:128, 32:33], tm[:, 0:16], AX.X, ALU.add)
                    b.inc(ins, "s7be")
                b.v(vg1)
                t3 = p.cnt.get("s7be", 0)
                def tg2(e, gidx=gidx, go=go, t3=t3):
                    b.w(e, "s7be", t3)
                    ins = e.transpose(p.PREP[0:1, 512 + go:512 + go + 128],
                                      tmv[0:128, 32:33], p.c_ident_f[:, :])
                    b.inc(ins, "s7bt")
                b.t(tg2)
            t4 = p.cnt.get("s7bt", 0)
            def vbc1(e, t4=t4):
                b.w(e, "s7bt", t4)
                ins = e.tensor_copy(p.srow[0:1, 0:512], p.PREP[0:1, 512:1024])
                b.inc(ins, "s7be")
            b.v(vbc1)
            t5 = p.cnt.get("s7be", 0)
            def tg3(e, t5=t5):
                b.w(e, "s7be", t5)
                ins = e.matmul(p.PAUX[:, 0:512], ones_f[:, :], p.srow[0:1, 0:512],
                               start=True, stop=True)
                b.inc(ins, "s7bt")
            b.t(tg3)
            t6 = p.cnt.get("s7bt", 0)
            def vbc2(e, t6=t6):
                b.w(e, "s7bt", t6)
                ins = e.tensor_copy(bc_rep[:, 0, :], p.PAUX[:, 0:512])
                b.inc(ins, "s7be")
            b.v(vbc2)
            t7 = p.cnt.get("s7be", 0)
            def vy(e, sp=sp, t7=t7):
                b.w(e, "s7be", t7)
                ins = None
                for c in range(DIC):
                    ins = e.tensor_tensor(dlt[:, c, :], dlt[:, c, :], bc_rep[:, 0, :],
                                          ALU.mult)
                b.vsync(e, ins)
                for c in range(DIC):
                    ins = e.scalar_tensor_tensor(ys[:, c, :], dlt[:, c, :],
                                                 Dv[:, c:c + 1], xcs[:, c, :],
                                                 ALU.add, ALU.mult)
                b.vsync(e, ins)
                for c in range(DIC):
                    ins = e.tensor_tensor(ys[:, c, :], ys[:, c, :], szs[:, c, :],
                                          ALU.mult)
                b.inc(ins, "s7y")
            b.v(vy)
            mb0 = o512 // 256
            xsegs = [(mb0 * SB_B + HALO, 0, 256),
                     ((mb0 + 1) * SB_B + HALO, 256, 256)]
            def yldx(e, sp=sp):
                for xoff, so, ln2 in xsegs:
                    b.dma(e, "s7xl", xres[:, :, so:so + ln2], x_ap(xin, xoff, ln2))
            b.y(yldx)
            for mh in range(2):
                wu_b = p.cnt.get("s7wu", 0)
                for k in range(DIC):
                    slot = k % 2
                    def gw(e, mh=mh, k=k, slot=slot, wu_b=wu_b):
                        if k >= 2:
                            b.w(e, "s7wu", wu_b + k - 1)
                        elif mh == 1 and k < 2:
                            b.w(e, "s7wu", wu_b)
                        b.dma(e, "s7ws", wo_slab[:, slot, :],
                              W["w_outT"][128 * k:128 * (k + 1), :])
                    b.g(gw)
                    def tw(e, mh=mh, k=k, slot=slot, sp=sp):
                        b.w(e, "s7ws")
                        if k == 0:
                            b.w(e, "s7y")
                            if mh == 1:
                                b.w(e, "s7oev")
                        ins = None
                        for m in range(4):
                            mm = 4 * mh + m
                            ins = e.matmul(p.P[m][:, 0:512],
                                           wo_slab[:, slot, 128 * mm:128 * (mm + 1)],
                                           ys[:, k, :], start=(k == 0),
                                           stop=(k == DIC - 1))
                        b.inc(ins, "s7wu")
                    b.t(tw)
                def vo(e, mh=mh):
                    b.w(e, "s7wu")
                    b.w(e, "s7xl")
                    ins = None
                    for m in range(4):
                        mm = 4 * mh + m
                        ins = e.tensor_tensor(xres[:, mm, :], xres[:, mm, :],
                                              p.P[m][:, 0:512], ALU.add)
                    b.inc(ins, "s7oev")
                b.v(vo)
            def yst2(e, sp=sp):
                b.w(e, "s7oev")
                for xoff, so, ln2 in xsegs:
                    b.dma(e, "s7xs", x_ap(xout, xoff, ln2), xres[:, :, so:so + ln2])
                b.w(e, "s7xs")
            b.y(yst2)
        p.retire_sems()



def emit_s3(p, xin, xout, W):
    ntok = p.ntok
    S = SEQ
    ones_row = p.c_onesrow_bf
    ones_col = p.c_ones_bf
    ones_f = p.c_ones_f

    emit_stats(p, "s3a", xin, ntok)

    wres = p.bf1[:, 0:32768].rearrange("p (w c m) -> p w c m", w=4, c=DC)
    brow = p.sr1[0:1, 0:2048]
    bq_pp = p.sbuf("s3_bqo", [128, 2 * DC], F32)
    ksv = p.sbuf("s3_ksv", [128, DC], BF16)
    with phase(p, "s3_wld") as b:
        def gl(e):
            for wi, nm in enumerate(["wqT", "wkT", "wvT", "woT"]):
                for k in range(DC):
                    b.dma(e, "s3wl", wres[:, wi, k, :],
                          W[nm][128 * k:128 * (k + 1), :])
            b.dma(e, "s3wl", brow[0:1, 0:1024], W["bk_row"])
            b.dma(e, "s3wl", brow[0:1, 1024:2048], W["bv_row"])
            b.dma(e, "s3wl", bq_pp[:, 0:DC], W["bq_pp"])
            b.dma(e, "s3wl", bq_pp[:, DC:2 * DC], W["bo_pp"])
            b.w(e, "s3wl")
        b.g(gl)

    ln_mb = p.bfv(p.bf2, 0, DC * S, DC)
    ktm = p.bfv(p.bf2, 16384, 1024, 1)
    vtm = p.bfv(p.bf2, 18432, 1024, 1)
    kvsb = p.bf2[:, 24576:25088]
    phq = p.bfv(p.bf2, 16384, DC * 1024, DC)
    scr1 = p.f1[:, 0:512]
    scr2 = p.f1[:, 512:1024]
    xts = p.f1[:, 1024:2048].rearrange("p (b t) -> p b t", b=2)
    denr = [p.f1[:, 2048:3072], p.f1[:, 3072:4096]]

    for mb in range(2):
        base = mb * S
        emit_ln(p, f"s3l{mb}", xin, ln_mb, tb_list(S, 512, base), buf_base=base)
        with phase(p, f"s3_kv{mb}") as b:
            for sc in range(16):
                so = 128 * sc
                def tb(e, sc=sc, so=so):
                    if sc >= 1:
                        b.w(e, "s3ke", 3 * sc)
                    ins = None
                    for half in range(2):
                        for kc in range(DC):
                            e.matmul(p.P[half][:, 0:512],
                                     ln_mb[:, kc, so:so + 128],
                                     wres[:, 1, kc, 512 * half:512 * half + 512],
                                     start=(kc == 0), stop=False)
                        e.matmul(p.P[half][:, 0:512], ones_row[:, 0:128],
                                 brow[0:1, 512 * half:512 * half + 512],
                                 start=False, stop=True)
                        for kc in range(DC):
                            e.matmul(p.P[2 + half][:, 0:512],
                                     ln_mb[:, kc, so:so + 128],
                                     wres[:, 2, kc, 512 * half:512 * half + 512],
                                     start=(kc == 0), stop=False)
                        ins = e.matmul(p.P[2 + half][:, 0:512], ones_row[:, 0:128],
                                       brow[0:1, 1024 + 512 * half:1024 + 512 * half + 512],
                                       start=False, stop=True)
                    b.inc(ins, "s3km")
                b.t(tb)
                def vphi(e, sc=sc):
                    b.w(e, "s3km", sc + 1)
                    if sc >= 1:
                        b.w(e, "s3kx", sc)
                    ins = e.tensor_scalar(scr1, p.P[0][:, 0:512], 0.0, None, ALU.min)
                    b.vsync(e, ins)
                    ins = e.tensor_scalar(scr2, p.P[1][:, 0:512], 0.0, None, ALU.min)
                    b.inc(ins, "s3kp")
                b.v(vphi)
                def sphi(e, sc=sc):
                    b.w(e, "s3kp", sc + 1)
                    if sc >= 1:
                        b.w(e, "s3kv", sc)
                    ins = e.activation(scr1, scr1, AF.Exp)
                    b.vsync(e, ins, "sself")
                    ins = e.activation(scr2, scr2, AF.Exp)
                    b.vsync(e, ins, "sself")
                    e.activation(vtm[:, 0, 0:512], p.P[2][:, 0:512], AF.Copy)
                    ins = e.activation(vtm[:, 0, 512:1024], p.P[3][:, 0:512], AF.Copy)
                    b.inc(ins, "s3kx")
                b.s(sphi)
                def vphi2(e, sc=sc):
                    b.w(e, "s3kx", sc + 1)
                    if sc >= 1:
                        b.w(e, "s3kv", sc)
                    ins = e.scalar_tensor_tensor(ktm[:, 0, 0:512], p.P[0][:, 0:512],
                                                 0.0, scr1, ALU.max, ALU.add)
                    ins = e.scalar_tensor_tensor(ktm[:, 0, 512:1024], p.P[1][:, 0:512],
                                                 0.0, scr2, ALU.max, ALU.add)
                    b.inc(ins, "s3ke", 3)
                b.v(vphi2)
                def tkv(e, sc=sc):
                    b.w(e, "s3ke", 3 * (sc + 1))
                    ins = None
                    for h in range(16):
                        pb = (h % 2) * 64
                        j = h // 2
                        e.matmul(p.PAUX[pb:pb + 64, 64 * j:64 * j + 64],
                                 ktm[:, 0, 64 * h:64 * h + 64],
                                 vtm[:, 0, 64 * h:64 * h + 64],
                                 start=(sc == 0), stop=(sc == 15))
                        ins = e.matmul(p.PAUX[pb:pb + 64, 512 + j:512 + j + 1],
                                       ktm[:, 0, 64 * h:64 * h + 64],
                                       ones_col[:, 0:1],
                                       start=(sc == 0), stop=(sc == 15))
                    b.inc(ins, "s3kv")
                b.t(tkv)
        p.retire_sems()
        with phase(p, f"s3_kve{mb}") as b:
            def se(e):
                ins = e.activation(kvsb, p.PAUX[:, 0:512], AF.Copy)
                b.inc(ins, "s3ev")
            b.s(se)
            def ve(e):
                ins = e.tensor_copy(ksv[:, :], p.PAUX[:, 512:512 + DC])
                b.inc(ins, "s3ev")
            b.v(ve)
            def tfin(e):
                b.w(e, "s3ev", 2)
            b.t(tfin)
        p.retire_sems()
        for hm in range(2):
            hbase = base + 1024 * hm
            with phase(p, f"s3_q{mb}_{hm}") as b:
                nt = 0
                for m in range(DC):
                    for t2 in range(2):
                        i = nt; nt += 1
                        bank = i % 4
                        to = 1024 * hm + 512 * t2
                        def tq2(e, m=m, to=to, bank=bank, i=i):
                            if i >= 4:
                                b.w(e, "s3qe", i - 3)
                            ins = None
                            for kc in range(DC):
                                ins = e.matmul(p.P[bank][:, 0:512],
                                               wres[:, 0, kc, 128 * m:128 * (m + 1)],
                                               ln_mb[:, kc, to:to + 512],
                                               start=(kc == 0), stop=(kc == DC - 1))
                            b.inc(ins, "s3qm")
                        b.t(tq2)
                        def vq(e, m=m, t2=t2, bank=bank, i=i):
                            b.w(e, "s3qm", i + 1)
                            if i >= 1:
                                b.w(e, "s3qs", i)
                            ins = e.tensor_scalar(scr1, p.P[bank][:, 0:512],
                                                  bq_pp[:, m:m + 1], None, ALU.add)
                            b.vsync(e, ins)
                            ins = e.tensor_scalar(scr2, scr1, 0.0, None, ALU.min)
                            b.inc(ins, "s3qp")
                        b.v(vq)
                        def sq(e, i=i):
                            b.w(e, "s3qp", i + 1)
                            ins = e.activation(scr2, scr2, AF.Exp)
                            b.inc(ins, "s3qs")
                        b.s(sq)
                        def vq2(e, m=m, t2=t2, bank=bank, i=i):
                            b.w(e, "s3qs", i + 1)
                            ins = e.scalar_tensor_tensor(phq[:, m, 512 * t2:512 * t2 + 512],
                                                         scr1, 0.0, scr2,
                                                         ALU.max, ALU.add)
                            b.inc(ins, "s3qe")
                        b.v(vq2)
            p.retire_sems()
            with phase(p, f"s3_at{mb}_{hm}") as b:
                nt = 0
                for j in range(DC):
                    for t2 in range(2):
                        i = nt; nt += 1
                        bank = i % 4
                        dslot = 512 * (i % 2)
                        def tn(e, j=j, t2=t2, bank=bank, i=i, dslot=dslot):
                            if i >= 4:
                                b.w(e, "s3ae", 2 * (i - 3))
                            if i >= 2:
                                b.w(e, "s3az", i - 1)
                            for pb in (0, 64):
                                e.matmul(p.P[bank][pb:pb + 64, 0:512],
                                         kvsb[pb:pb + 64, 64 * j:64 * j + 64],
                                         phq[pb:pb + 64, j, 512 * t2:512 * t2 + 512],
                                         start=True, stop=True)
                            ins = None
                            for pb in (0, 64):
                                ins = e.matmul(p.PAUX[pb:pb + 1, dslot:dslot + 512],
                                               ksv[pb:pb + 64, j:j + 1],
                                               phq[pb:pb + 64, j, 512 * t2:512 * t2 + 512],
                                               start=True, stop=True)
                            b.inc(ins, "s3am")
                        b.t(tn)
                        def vz(e, j=j, t2=t2, bank=bank, i=i, dslot=dslot):
                            b.w(e, "s3am", i + 1)
                            dr = denr[i % 2]
                            ins = e.tensor_scalar(dr[0:1, 0:512],
                                                  p.PAUX[0:1, dslot:dslot + 512],
                                                  1.0, 1e-6, ALU.mult, ALU.add)
                            ins2 = e.tensor_scalar(dr[0:1, 512:1024],
                                                   p.PAUX[64:65, dslot:dslot + 512],
                                                   1.0, 1e-6, ALU.mult, ALU.add)
                            b.vsync(e, ins2)
                            ins = e.reciprocal(dr[0:1, 0:1024], dr[0:1, 0:1024])
                            b.inc(ins, "s3az")
                        b.v(vz)
                        def trep(e, j=j, t2=t2, bank=bank, i=i):
                            b.w(e, "s3az", i + 1)
                            if i >= 2:
                                b.w(e, "s3ae", 2 * (i - 1))
                            dr = denr[i % 2]
                            pslot = 512 * (i % 2)
                            e.matmul(p.PREP[0:64, pslot:pslot + 512], ones_f[0:1, 0:64],
                                     dr[0:1, 0:512], start=True, stop=True)
                            ins = e.matmul(p.PREP[64:128, pslot:pslot + 512],
                                           ones_f[0:1, 0:64], dr[0:1, 512:1024],
                                           start=True, stop=True)
                            b.inc(ins, "s3ar")
                        b.t(trep)
                        def sn(e, j=j, t2=t2, bank=bank, i=i):
                            b.w(e, "s3ar", i + 1)
                            ins = e.activation(phq[:, j, 512 * t2:512 * t2 + 512],
                                               p.P[bank][:, 0:512], AF.Copy)
                            b.inc(ins, "s3ac")
                        b.s(sn)
                        def va(e, j=j, t2=t2, bank=bank, i=i):
                            b.w(e, "s3ac", i + 1)
                            pslot = 512 * (i % 2)
                            sl = phq[:, j, 512 * t2:512 * t2 + 512]
                            ins = e.tensor_tensor(sl, sl, p.PREP[:, pslot:pslot + 512],
                                                  ALU.mult)
                            b.inc(ins, "s3ae", 2)
                        b.v(va)
            p.retire_sems()
            with phase(p, f"s3_o{mb}_{hm}") as b:
                nt = 0
                for m in range(DC):
                    for t2 in range(2):
                        i = nt; nt += 1
                        bank = i % 4
                        off = hbase + 512 * t2
                        def yx(e, m=m, off=off, i=i):
                            if i >= 2:
                                b.w(e, "s3os", 16 * (i - 1))
                            b.dma(e, "s3xl",
                                  xts[:, i % 2, :],
                                  xin[128 * m:128 * (m + 1), off:off + 512])
                        b.y(yx)
                        def to(e, m=m, t2=t2, bank=bank, i=i):
                            if i >= 4:
                                b.w(e, "s3oe", i - 3)
                            ins = None
                            for kc in range(DC):
                                ins = e.matmul(p.P[bank][:, 0:512],
                                               wres[:, 3, kc, 128 * m:128 * (m + 1)],
                                               phq[:, kc, 512 * t2:512 * t2 + 512],
                                               start=(kc == 0), stop=(kc == DC - 1))
                            b.inc(ins, "s3om")
                        b.t(to)
                        def vo(e, m=m, bank=bank, i=i):
                            b.w(e, "s3om", i + 1)
                            b.w(e, "s3xl", 16 * (i + 1))
                            ins = e.scalar_tensor_tensor(xts[:, i % 2, :],
                                                         p.P[bank][:, 0:512],
                                                         bq_pp[:, DC + m:DC + m + 1],
                                                         xts[:, i % 2, :],
                                                         ALU.add, ALU.add)
                            b.inc(ins, "s3oe")
                        b.v(vo)
                        def yo(e, m=m, off=off, i=i):
                            b.w(e, "s3oe", i + 1)
                            b.dma(e, "s3os", xout[128 * m:128 * (m + 1), off:off + 512],
                                  xts[:, i % 2, :])
                        b.y(yo)
                def yfin(e):
                    b.w(e, "s3os")
                b.y(yfin)
            p.retire_sems()


def build_launch_A(steps):
    p = Prog("launchA_" + "".join(steps), NT_A, launch="A")
    nc = p.nc
    p.emit_const_init()
    xio_in = nc.dram_tensor("x", [D, NT_A], F32, kind="ExternalInput").ap()
    xio_out = nc.dram_tensor("xo", [D, NT_A], F32, kind="ExternalOutput").ap()
    inter = [nc.dram_tensor(f"xmid{i}", [D, NT_A], F32).ap()
             for i in range(len(steps) - 1)]
    chain = [xio_in] + inter + [xio_out]
    wdecl = {}

    def dram(name, shape):
        if name not in wdecl:
            wdecl[name] = nc.dram_tensor(name, list(shape), F32,
                                         kind="ExternalInput").ap()
        return wdecl[name]

    moe_state = {}
    for si, st in enumerate(steps):
        p.retire_sems()
        xin, xout = chain[si], chain[si + 1]
        if st == "s3":
            W = {
                "wqT": dram("s3_wqT", [D, D]),
                "wkT": dram("s3_wkT", [D, D]),
                "wvT": dram("s3_wvT", [D, D]),
                "woT": dram("s3_woT", [D, D]),
                "bk_row": dram("s3_bk_row", [1, D]),
                "bv_row": dram("s3_bv_row", [1, D]),
                "bq_pp": dram("s3_bq_pp", [128, DC]),
                "bo_pp": dram("s3_bo_pp", [128, DC]),
            }
            emit_s3(p, xin, xout, W)
        elif st in ("s2", "s4"):
            W = {
                "routerT": dram("s2_routerT", [D, 8]),
                "router_bias": dram("s2_router_bias", [8, 1]),
                "gu": dram("s2_gu", [8, D, 4096]),
                "gub": dram("s2_gub", [8, 128, 32]),
                "down": dram("s2_down", [8, 2048, D]),
                "downb_row": dram("s2_downb_row", [1, 8 * D]),
            }
            emit_s2(p, xin, xout, W, tag=st, state=moe_state)
        elif st == "s1":
            W = {
                "wq": dram("s1_wq", [D, D]),
                "wk": dram("s1_wk", [D, D]),
                "wv": dram("s1_wv", [D, D]),
                "woT": dram("s1_woT", [D, D]),
                "bq_pp": dram("s1_bq_pp", [128, DC]),
                "bk_row": dram("s1_bk_row", [1, D]),
                "bv_row": dram("s1_bv_row", [1, D]),
                "bo_pp": dram("s1_bo_pp", [128, DC]),
                "g1": dram("s1_g1", [128, DC]),
                "b1ln": dram("s1_b1ln", [128, DC]),
                "g2": dram("s1_g2", [128, DC]),
                "b2ln": dram("s1_b2ln", [128, DC]),
                "w1": dram("s1_w1", [D, 4096]),
                "b1_pp": dram("s1_b1_pp", [128, 32]),
                "w2": dram("s1_w2", [4096, D]),
                "b2_pp": dram("s1_b2_pp", [128, DC]),
            }
            emit_s1(p, xin, xout, W)
        else:
            raise ValueError(st)
    p.close()
    return p


def prep_inputs_A(inputs, x0, steps=("s1", "s2", "s3", "s4")):
    W = inputs
    shared = {}
    if "s3" in steps:
        shared.update({
            "s3_wqT": np.ascontiguousarray(W["L0_attn_b_Wq"].T),
            "s3_wkT": np.ascontiguousarray(W["L0_attn_b_Wk"].T),
            "s3_wvT": np.ascontiguousarray(W["L0_attn_b_Wv"].T),
            "s3_woT": np.ascontiguousarray(W["L0_attn_b_Wo"].T),
            "s3_bk_row": np.ascontiguousarray(W["L0_attn_b_bk"].reshape(1, D)),
            "s3_bv_row": np.ascontiguousarray(W["L0_attn_b_bv"].reshape(1, D)),
            "s3_bq_pp": host_vec(W["L0_attn_b_bq"]),
            "s3_bo_pp": host_vec(W["L0_attn_b_bo"]),
        })
    if "s2" in steps or "s4" in steps:
        shared.update({
            "s2_routerT": np.ascontiguousarray(W["L0_ffn_router_weight"].T),
            "s2_router_bias": np.ascontiguousarray(
                W["L0_ffn_router_bias"].reshape(8, 1)),
            "s2_gu": np.ascontiguousarray(
                W["L0_ffn_gate_up_proj"].transpose(0, 2, 1)),
            "s2_gub": np.stack([host_vec(W["L0_ffn_gate_up_proj_bias"][e])
                                for e in range(8)]),
            "s2_down": np.ascontiguousarray(
                W["L0_ffn_down_proj"].transpose(0, 2, 1)),
            "s2_downb_row": np.ascontiguousarray(
                np.asarray(W["L0_ffn_down_proj_bias"],
                           np.float32).reshape(1, 8 * D)),
        })
    if "s1" in steps:
        shared.update({
            "s1_wq": np.ascontiguousarray(W["L0_attn_a_Wq"].reshape(D, D)),
            "s1_wk": np.ascontiguousarray(W["L0_attn_a_Wk"].reshape(D, D)),
            "s1_wv": np.ascontiguousarray(W["L0_attn_a_Wv"].reshape(D, D)),
            "s1_woT": np.ascontiguousarray(W["L0_attn_a_Wo"].reshape(D, D)),
            "s1_bq_pp": host_vec(W["L0_attn_a_bq"]),
            "s1_bk_row": np.ascontiguousarray(W["L0_attn_a_bk"].reshape(1, D)),
            "s1_bv_row": np.ascontiguousarray(W["L0_attn_a_bv"].reshape(1, D)),
            "s1_bo_pp": host_vec(W["L0_attn_a_bo"]),
            "s1_g1": host_vec(W["L0_attn_a_ln1_scale"]),
            "s1_b1ln": host_vec(W["L0_attn_a_ln1_bias"]),
            "s1_g2": host_vec(W["L0_attn_a_ln2_scale"]),
            "s1_b2ln": host_vec(W["L0_attn_a_ln2_bias"]),
            "s1_w1": np.ascontiguousarray(W["L0_attn_a_W1"]),
            "s1_b1_pp": host_vec(W["L0_attn_a_b1"]),
            "s1_w2": np.ascontiguousarray(W["L0_attn_a_W2"]),
            "s1_b2_pp": host_vec(W["L0_attn_a_b2"]),
        })
    in_maps = []
    for c in range(NCORES):
        xs = x0[2 * c:2 * c + 2]
        xfm = np.ascontiguousarray(xs.reshape(NT_A, D).T)
        m = {"x": xfm}
        m.update(shared)
        in_maps.append(m)
    return in_maps


def finish_outputs_A(results):
    x4 = np.empty((NMB, SEQ, D), dtype=np.float32)
    for c, r in enumerate(results):
        xo = np.asarray(r["xo"])
        x4[2 * c:2 * c + 2] = xo.reshape(D, 2, SEQ).transpose(1, 2, 0)
    return x4



def emit_s2(p, xin, xout, W, tag, state):
    nc = p.nc
    ntok = p.ntok
    ones_f = p.c_ones_f
    onescol_f = p.c_onescol_f
    ident_f = p.c_ident_f
    first = "gu_bf" not in state
    if first:
        state["gu_bf"] = nc.dram_tensor("moe_gu_bf", [8, D, 4096], BF16)
        state["down_bf"] = nc.dram_tensor("moe_down_bf", [8, 2048, D], BF16)
        state["w_dram"] = nc.dram_tensor("moe_w", [8, ntok], BF16)
        state["dbrow_bf"] = nc.dram_tensor("moe_dbrow_bf", [1, 8 * D], BF16)
        state["rt"] = p.sbuf("s2_rt", [128, DC, 8], F32)
        state["gub_all"] = p.sbuf("s2_gub_sb", [128, 256], F32)
        state["sR"] = p.sbuf("s2_sR", [8, 1], F32)
        state["rb"] = p.sbuf("s2_rb", [8, 1], F32)
    gu_bf = state["gu_bf"].ap()
    down_bf = state["down_bf"].ap()
    w_dram = state["w_dram"]
    dbrow_bf = state["dbrow_bf"].ap()
    rt, gub_all = state["rt"], state["gub_all"]
    sR, rb = state["sR"], state["rb"]

    emit_stats(p, tag + "a", xin, ntok)

    if first:
        with phase(p, "s2_cast") as b:
            def gcast(e):
                b.dma(e, "mcast", dbrow_bf[0:1, :], W["downb_row"])
                for e8 in range(8):
                    for h in range(4):
                        b.dma(e, "mcast", gu_bf[e8, 256 * h:256 * (h + 1), :],
                              W["gu"][e8, 256 * h:256 * (h + 1), :])
                    for h in range(2):
                        b.dma(e, "mcast", down_bf[e8, 1024 * h:1024 * (h + 1), :],
                              W["down"][e8, 1024 * h:1024 * (h + 1), :])
            b.g(gcast)
            def yini(e):
                b.dma(e, "s2ini", rt[:, :, :],
                      W["routerT"].rearrange("(c p) e -> p c e", p=128))
                for e8 in range(8):
                    b.dma(e, "s2ini", gub_all[:, 32 * e8:32 * (e8 + 1)],
                          W["gub"][e8])
                b.dma(e, "s2ini", rb[:, :], W["router_bias"])
            b.y(yini)

    def w_mcast_gu(e, b, e8):
        if first:
            b.w(e, "mcast", 96 * e8 + 80)

    def w_mcast_down(e, b, e8):
        if first:
            b.w(e, "mcast", 96 * (e8 + 1) + 16)

    logits = p.f2[0:8, 0:ntok]
    xr = p.f1[:, 0:4096].rearrange("p (c t) -> p c t", c=DC)
    s_mm, s_v = tag + "rm", tag + "rv"
    with phase(p, tag + "_router") as b:
        def t0(e):
            if first:
                b.w(e, "s2ini")
            ins = None
            for k in range(DC):
                ins = e.matmul(p.PAUX[0:8, 0:1], rt[:, k, :], onescol_f[:, :],
                               start=(k == 0), stop=(k == DC - 1))
            b.inc(ins, s_mm)
        b.t(t0)
        def v0(e):
            b.w(e, s_mm, 1)
            ins = e.tensor_copy(sR[:, :], p.PAUX[0:8, 0:1])
            b.vsync(e, ins)
        b.v(v0)
        for i, (off, lnn) in enumerate(tb_list(ntok, 512)):
            slot = i % 2
            def ybody(e, i=i, off=off, lnn=lnn):
                if i >= 1:
                    b.w(e, s_v, i)
                b.dma(e, tag + "rx", xr[:, :, 0:lnn], x_ap(xin, off, lnn))
            b.y(ybody)
            def tbody(e, i=i, lnn=lnn, slot=slot):
                if i >= 2:
                    b.w(e, s_v, i - 1)
                b.w(e, tag + "rx")
                ins = None
                for k in range(DC):
                    ins = e.matmul(p.PAUX[0:8, 512 * slot:512 * slot + lnn],
                                   rt[:, k, :], xr[:, k, 0:lnn],
                                   start=(k == 0), stop=(k == DC - 1))
                b.inc(ins, s_mm)
            b.t(tbody)
            def vbody(e, i=i, off=off, lnn=lnn, slot=slot):
                b.w(e, s_mm, i + 2)
                tmp = logits[:, off:off + lnn]
                ins = e.tensor_scalar(tmp, p.smu[0:8, off:off + lnn], sR[:, 0:1],
                                      None, ALU.mult)
                b.vsync(e, ins)
                ins = e.tensor_tensor(tmp, p.PAUX[0:8, 512 * slot:512 * slot + lnn],
                                      tmp, ALU.subtract)
                b.vsync(e, ins)
                ins = e.tensor_tensor(tmp, tmp, p.sr[0:8, off:off + lnn], ALU.mult)
                b.vsync(e, ins)
                ins = e.tensor_scalar(tmp, tmp, rb[:, 0:1], None, ALU.add)
                b.inc(ins, s_v)
            b.v(vbody)

    wfm = p.bf2[0:8, 0:ntok]
    TM = p.f1[:, 4096:4256]
    with phase(p, tag + "_top4") as b:
        ntile = ntok // 128
        s_tr, s_w = tag + "tr", tag + "tw"
        for i in range(ntile):
            o = 128 * i
            pslot = i % 2
            def tbody(e, i=i, o=o, pslot=pslot):
                if i >= 2:
                    b.w(e, s_w, i - 1)
                ins = e.transpose(p.PAUX[0:128, 512 * pslot:512 * pslot + 8],
                                  logits[:, o:o + 128], ident_f[0:8, 0:8])
                b.inc(ins, s_tr)
            b.t(tbody)
            def vsbody(e, i=i, pslot=pslot):
                b.w(e, s_tr, i + 1)
                if i >= 1:
                    b.w(e, tag + "tc", i)
                lt = TM[:, 0:8]
                wk = TM[:, 8:16]
                mx = TM[:, 16:17]
                m1n = TM[:, 18:19]
                msk = TM[:, 32:40]
                ins = e.tensor_copy(lt, p.PAUX[0:128, 512 * pslot:512 * pslot + 8])
                b.vsync(e, ins)
                ins = e.tensor_copy(wk, lt)
                b.vsync(e, ins)
                ins = e.reduce_max(mx, wk, AX.X)
                b.vsync(e, ins)
                ins = e.tensor_scalar(m1n, mx, -1.0, None, ALU.mult)
                b.vsync(e, ins)
                for r in range(3):
                    ins = e.reduce_max(mx, wk, AX.X)
                    b.vsync(e, ins)
                    ins = e.tensor_scalar(msk, wk, mx, None, ALU.is_ge)
                    b.vsync(e, ins)
                    ins = e.scalar_tensor_tensor(wk, msk, BIG_NEG, wk,
                                                 ALU.mult, ALU.add)
                    b.vsync(e, ins)
                ins = e.reduce_max(mx, wk, AX.X)
                b.vsync(e, ins)
                ins = e.tensor_scalar(msk, lt, mx, None, ALU.is_ge)
                b.inc(ins, tag + "tm")
            b.v(vsbody)
            def sbody(e, i=i):
                b.w(e, tag + "tm", i + 1)
                if i >= 1:
                    b.w(e, tag + "tt", i)
                ins = e.activation(TM[:, 24:32], TM[:, 0:8], AF.Exp,
                                   bias=TM[:, 18:19])
                b.inc(ins, tag + "tp")
            b.s(sbody)
            def v2body(e, i=i):
                b.w(e, tag + "tp", i + 1)
                pp = TM[:, 24:32]
                msk = TM[:, 32:40]
                sm = TM[:, 19:20]
                ins = e.tensor_tensor(pp, pp, msk, ALU.mult)
                b.vsync(e, ins)
                ins = e.tensor_reduce(sm, pp, AX.X, ALU.add)
                b.vsync(e, ins)
                ins = e.reciprocal(sm, sm)
                b.vsync(e, ins)
                ins = e.tensor_scalar(pp, pp, sm, None, ALU.mult)
                b.inc(ins, tag + "tc")
            b.v(v2body)
            def tbody2(e, i=i, pslot=pslot):
                b.w(e, tag + "tc", i + 1)
                ins = e.transpose(p.PAUX[0:8, 512 * pslot + 16:512 * pslot + 144],
                                  TM[:, 24:32], ident_f[:, :])
                b.inc(ins, tag + "tt")
            b.t(tbody2)
            def v3body(e, i=i, o=o, pslot=pslot):
                b.w(e, tag + "tt", i + 1)
                ins = e.tensor_copy(wfm[:, o:o + 128],
                                    p.PAUX[0:8, 512 * pslot + 16:512 * pslot + 144])
                b.inc(ins, s_w)
            b.v(v3body)
        def ydump(e):
            b.w(e, s_w)
            b.dma(e, tag + "wd", w_dram.ap()[0:8, 0:ntok], wfm)
            b.w(e, tag + "wd")
        b.y(ydump)

    ln_q = p.bfv(p.bf2, 0, 8192, DC)
    act_t = p.bfv(p.bf2, 8192, 16384, 16)
    dslab = p.bfv(p.bf2, 24576, 1024, 2)
    wrep = p.sr1[:, 0:2048].rearrange("p (s t) -> p s t", s=2)
    gu_res = p.bfv(p.bf1, 0, 32768, DC)
    y_q = p.f2[:, 0:8192].rearrange("p (c t) -> p c t", c=DC)
    SC = [[p.f1[:, 1024 * j + 512 * s:1024 * j + 512 * s + 512]
           for s in range(2)] for j in range(4)]

    def gu_issue(e, b, e8, q, blkset):
        sem = tag + ("ga" if blkset == 0 else "gb")
        for blk in (0, 2) if blkset == 0 else (1, 3):
            for k in range(DC):
                b.dma(e, sem, gu_res[:, k, 1024 * blk:1024 * (blk + 1)],
                      gu_bf[e8, 128 * k:128 * (k + 1),
                            1024 * blk:1024 * (blk + 1)])

    for q in range(4):
        qo = 1024 * q
        ys_base = p.cnt.get(tag + "ys", 0)

        def pre(b, q=q):
            def sc_pre(e, q=q):
                w_mcast_gu(e, b, 0)
                gu_issue(e, b, 0, q, 0)
            b.s(sc_pre)
        emit_ln(p, f"{tag}l{q}", xin, ln_q, tb_list(1024, 512, qo),
                buf_base=qo, pre=pre)

        with phase(p, f"{tag}q{q}") as b:
            s_ga, s_gb = tag + "ga", tag + "gb"
            s_gmm, s_gpr = tag + "gmm", tag + "gpr"
            s_gv1, s_gsg, s_ggp, s_gae = (tag + "gv1", tag + "gsg",
                                          tag + "ggp", tag + "gae")
            s_wr, s_ds, s_dk, s_dwm = (tag + "wr", tag + "ds",
                                       tag + "dk", tag + "dwm")
            s_db = tag + "db"
            s_ye, s_yx, s_ys = tag + "ye", tag + "yx", tag + "ys"

            def sc_x(e, q=q, ys_base=ys_base):
                b.w(e, s_ys, ys_base)
                for tb2 in range(2):
                    b.dma(e, s_yx, y_q[:, :, 512 * tb2:512 * (tb2 + 1)],
                          x_ap(xin, qo + 512 * tb2, 512))
                w_mcast_gu(e, b, 0)
                gu_issue(e, b, 0, q, 1)
            b.s(sc_x)

            def sy_wrep(e, el):
                if el >= 2:
                    b.w(e, s_dwm, 2 * (el - 1))
                b.dma(e, s_wr, wrep[:, el % 2, :],
                      bass.AP(w_dram, el * ntok + qo, [[0, 128], [1, 1024]]))
                b.dma(e, s_db, p.sr1[0:1, 2048 + 1024 * (el % 2):
                                     3072 + 1024 * (el % 2)],
                      dbrow_bf[0:1, 1024 * el:1024 * (el + 1)])
            def sy_slabs(e, el):
                w_mcast_down(e, b, el)
                for dg in range(2):
                    for ki in range(16):
                        s_ = 32 * el + 16 * dg + ki
                        if s_ >= 2:
                            b.w(e, s_dk, s_ - 1)
                        b.dma(e, s_ds, dslab[:, s_ % 2, :],
                              down_bf[el, 128 * ki:128 * (ki + 1),
                                      512 * dg:512 * (dg + 1)])
            def sy_all(e):
                sy_wrep(e, 0)
                sy_wrep(e, 1)
                for el in range(8):
                    sy_slabs(e, el)
                    if el + 2 <= 7:
                        sy_wrep(e, el + 2)
                b.w(e, s_ye, 128)
                for tb2 in range(2):
                    b.dma(e, s_ys, x_ap(xout, qo + 512 * tb2, 512),
                          y_q[:, :, 512 * tb2:512 * (tb2 + 1)])
            b.y(sy_all)

            acb = [[p.P[0][:, 0:512], p.P[1][:, 0:512]],
                   [p.P[2][:, 0:512], p.P[3][:, 0:512]],
                   [p.PREP[:, 0:512], p.PREP[:, 512:1024]],
                   [p.PAUX[:, 0:512], p.PAUX[:, 512:1024]]]

            for el in range(8):
                for ic in range(16):
                    gset = (acb[0] + acb[1]) if ic % 2 == 0 else (acb[2] + acb[3])
                    def tgu(e, el=el, ic=ic, gset=gset):
                        if ic == 0:
                            b.w(e, s_ye, 16 * el)
                            b.w(e, s_ga, 256 * (el + 1))
                        if ic == 8:
                            b.w(e, s_gb, 256 * (el + 1))
                        if ic >= 2:
                            b.w(e, s_gpr, 32 * el + 2 * ic - 2)
                        ins = None
                        for gi in range(4):
                            col = 128 * ic + (2048 if gi >= 2 else 0)
                            to = 512 * (gi % 2)
                            for k in range(DC):
                                ins = e.matmul(gset[gi],
                                               gu_res[:, k, col:col + 128],
                                               ln_q[:, k, to:to + 512],
                                               start=(k == 0), stop=(k == DC - 1))
                        b.inc(ins, s_gmm)
                    b.t(tgu)
                    for tb2 in range(2):
                        u = 2 * ic + tb2
                        def v1(e, el=el, ic=ic, tb2=tb2, u=u, gset=gset):
                            b.w(e, s_gmm, 16 * el + ic + 1)
                            if ic >= 1:
                                b.w(e, s_ggp, 32 * el + 2 * (ic - 1) + tb2 + 1)
                            gs, us = SC[0][tb2], SC[1][tb2]
                            e.tensor_scalar(gs, gset[tb2],
                                            gub_all[:, 32 * el + ic:32 * el + ic + 1],
                                            7.0, ALU.add, ALU.min)
                            ins = e.tensor_scalar(us, gset[2 + tb2],
                                                  gub_all[:, 32 * el + 16 + ic:32 * el + 17 + ic],
                                                  7.0, ALU.add, ALU.min)
                            b.vsync(e, ins)
                            ins = e.tensor_scalar(us, us, -7.0, 1.0,
                                                  ALU.max, ALU.add)
                            b.inc(ins, s_gpr)
                        b.v(v1)
                        def sg(e, el=el, ic=ic, tb2=tb2, u=u):
                            b.w(e, s_gpr, 32 * el + u + 1)
                            if ic >= 1:
                                b.w(e, s_ggp, 32 * el + 2 * (ic - 1) + tb2 + 1)
                            gs, ss = SC[0][tb2], SC[2][tb2]
                            ins = e.activation(ss, gs, AF.Sigmoid, scale=1.702)
                            b.inc(ins, s_gsg)
                            if ic == 7 and tb2 == 1 and el + 1 <= 7:
                                b.w(e, s_gmm, 16 * el + 8)
                                w_mcast_gu(e, b, el + 1)
                                gu_issue(e, b, el + 1, q, 0)
                            if ic == 15 and tb2 == 1 and el + 1 <= 7:
                                b.w(e, s_gmm, 16 * (el + 1))
                                gu_issue(e, b, el + 1, q, 1)
                        b.s(sg)
                        def gg(e, el=el, ic=ic, tb2=tb2, u=u):
                            b.w(e, s_gsg, 32 * el + u + 1)
                            if ic >= 1:
                                b.w(e, s_gae, 32 * el + 2 * (ic - 1) + tb2 + 1)
                            elif el >= 1:
                                b.w(e, s_gae, 32 * el)
                            gs, us, ss, t1 = (SC[0][tb2], SC[1][tb2],
                                              SC[2][tb2], SC[3][tb2])
                            ins = e.tensor_tensor(t1, gs, ss, ALU.mult)
                            b.vsync(e, ins, "gself")
                            ins = e.tensor_tensor(t1, t1, us, ALU.mult)
                            b.inc(ins, s_ggp)
                        b.g(gg)
                        def v2(e, el=el, ic=ic, tb2=tb2, u=u):
                            b.w(e, s_ggp, 32 * el + u + 1)
                            if u == 0:
                                b.w(e, s_wr, 16 * (el + 1))
                            t1 = SC[3][tb2]
                            ins = e.tensor_tensor(
                                act_t[:, ic, 512 * tb2:512 * (tb2 + 1)], t1,
                                wrep[:, el % 2, 512 * tb2:512 * (tb2 + 1)],
                                ALU.mult)
                            b.inc(ins, s_gae)
                        b.v(v2)
                for dg in range(2):
                    def tdn(e, el=el, dg=dg):
                        if dg == 0:
                            b.w(e, s_gae, 32 * (el + 1))
                        else:
                            b.w(e, s_ye, 16 * el + 8)
                        for ki in range(16):
                            s_ = 32 * el + 16 * dg + ki
                            b.w(e, s_ds, 16 * (s_ + 1))
                            ins = None
                            for dm in range(4):
                                for tb2 in range(2):
                                    ins = e.matmul(
                                        acb[dm][tb2],
                                        dslab[:, s_ % 2, 128 * dm:128 * (dm + 1)],
                                        act_t[:, ki, 512 * tb2:512 * (tb2 + 1)],
                                        start=(ki == 0), stop=False)
                            b.inc(ins, s_dk)
                        if dg == 0:
                            b.w(e, s_db, 16 * (el + 1))
                        ins = None
                        for dm in range(4):
                            for tb2 in range(2):
                                col = 2048 + 1024 * (el % 2) + 128 * (4 * dg + dm)
                                ins = e.matmul(
                                    acb[dm][tb2],
                                    p.sr1[0:1, col:col + 128],
                                    p.sr1[0:1, 1024 * (el % 2) + 512 * tb2:
                                          1024 * (el % 2) + 512 * tb2 + 512],
                                    start=False, stop=True)
                        b.inc(ins, s_dwm)
                    b.t(tdn)
                    def vye(e, el=el, dg=dg):
                        b.w(e, s_dwm, 2 * el + dg + 1)
                        if el == 0 and dg == 0:
                            b.w(e, s_yx, 32)
                        ins = None
                        for dm in range(4):
                            for tb2 in range(2):
                                sl = y_q[:, 4 * dg + dm,
                                         512 * tb2:512 * (tb2 + 1)]
                                ins = e.tensor_tensor(sl, sl, acb[dm][tb2],
                                                      ALU.add)
                                b.inc(ins, s_ye)
                    b.v(vye)
        p.retire_sems(keep=(tag + "ys", "mcast"))

    with phase(p, tag + "_fin") as b:
        def vf(e):
            b.w(e, tag + "ys")
        b.v(vf)
    p.retire_sems(keep=("mcast",))


def emit_s1(p, xin, xout, W):
    nc = p.nc
    ntok = p.ntok
    S = SEQ
    ones_row = p.c_onesrow_bf
    ones_col = p.c_ones_bf
    ones_f = p.c_ones_f

    xres_d = nc.dram_tensor("s1_xres_d", [D, NT_A], F32).ap()
    xmid_d = nc.dram_tensor("s1_xmid_d", [D, NT_A], F32).ap()
    w1_bf = nc.dram_tensor("s1_w1bf", [D, 4096], BF16).ap()
    w2_bf = nc.dram_tensor("s1_w2bf", [4096, D], BF16).ap()

    emit_stats(p, "s1a", xin, ntok, with_r1=True)

    wres = p.bf1[:, 0:32768].rearrange("p (w c m) -> p w c m", w=4, c=DC)
    bq_pp = p.alloc_vec(DC)
    bo_pp = p.alloc_vec(DC)
    g1v = p.alloc_vec(DC)
    b1v = p.alloc_vec(DC)
    g2v = p.alloc_vec(DC)
    b2v = p.alloc_vec(DC)
    b2_pp = p.alloc_vec(DC)
    b1_pp = p.alloc_vec(32)
    ksv = p.sbuf("s3_ksv", [128, DC], BF16)
    brow = p.bf2[0:1, 19456:21504]

    with phase(p, "s1_wld") as b:
        def gl(e):
            for wi, nm in enumerate(["wk", "wv", "wq", "woT"]):
                for k in range(DC):
                    b.dma(e, "s1wl", wres[:, [1, 2, 0, 3][wi], k, :],
                          W[nm][128 * k:128 * (k + 1), :])
            for h in range(4):
                b.dma(e, "s1wc", w1_bf[256 * h:256 * (h + 1), :],
                      W["w1"][256 * h:256 * (h + 1), :])
            for h in range(4):
                b.dma(e, "s1wc", w2_bf[1024 * h:1024 * (h + 1), :],
                      W["w2"][1024 * h:1024 * (h + 1), :])
        b.g(gl)
        def yv(e):
            b.dma(e, "s1vl", bq_pp, W["bq_pp"])
            b.dma(e, "s1vl", bo_pp, W["bo_pp"])
            b.dma(e, "s1vl", g1v, W["g1"])
            b.dma(e, "s1vl", b1v, W["b1ln"])
            b.dma(e, "s1vl", g2v, W["g2"])
            b.dma(e, "s1vl", b2v, W["b2ln"])
            b.dma(e, "s1vl", b2_pp, W["b2_pp"])
            b.dma(e, "s1vl", b1_pp, W["b1_pp"])
            b.w(e, "s1vl")
        b.y(yv)

    ln_mb = p.bfv(p.bf2, 0, DC * S, DC)
    ktm = p.bfv(p.bf2, 16384, 1024, 1)
    vtm = p.bfv(p.bf2, 18432, 1024, 1)
    kvsb = p.bf2[:, 24576:25088]
    phq = p.bfv(p.bf2, 16384, DC * 1024, DC)
    scr1 = p.f1[:, 0:512]
    scr2 = p.f1[:, 512:1024]
    lnx = p.f1[:, 0:1024].rearrange("p (b t) -> p b t", b=2)
    xts = p.f1[:, 1024:2048].rearrange("p (b t) -> p b t", b=2)
    xrs = p.f1[:, 2048:3072].rearrange("p (b t) -> p b t", b=2)
    denr = [p.f1[:, 2048:3072], p.f1[:, 3072:4096]]

    for mb in range(2):
        base = mb * S
        emit_ln(p, f"s1l{mb}", xin, ln_mb, tb_list(S, 512, base),
                buf_base=base, r_arena=p.sr1, gb=(g1v, b1v))
        with phase(p, f"s1_kv{mb}") as b:
            def gbrow(e):
                b.dma(e, "s1br", brow[0:1, 0:1024], W["bk_row"])
                b.dma(e, "s1br", brow[0:1, 1024:2048], W["bv_row"])
            b.g(gbrow)
            for sc in range(16):
                so = 128 * sc
                def tb(e, sc=sc, so=so):
                    if sc == 0:
                        b.w(e, "s1wl", 256)
                        b.w(e, "s1br")
                    if sc >= 1:
                        b.w(e, "s1ke", 2 * sc)
                    ins = None
                    for half in range(2):
                        for kc in range(DC):
                            e.matmul(p.P[half][:, 0:512],
                                     ln_mb[:, kc, so:so + 128],
                                     wres[:, 1, kc, 512 * half:512 * half + 512],
                                     start=(kc == 0), stop=False)
                        e.matmul(p.P[half][:, 0:512], ones_row[:, 0:128],
                                 brow[0:1, 512 * half:512 * half + 512],
                                 start=False, stop=True)
                        for kc in range(DC):
                            e.matmul(p.P[2 + half][:, 0:512],
                                     ln_mb[:, kc, so:so + 128],
                                     wres[:, 2, kc, 512 * half:512 * half + 512],
                                     start=(kc == 0), stop=False)
                        ins = e.matmul(p.P[2 + half][:, 0:512], ones_row[:, 0:128],
                                       brow[0:1, 1024 + 512 * half:1536 + 512 * half],
                                       start=False, stop=True)
                    b.inc(ins, "s1km")
                b.t(tb)
                def vphi(e, sc=sc):
                    b.w(e, "s1km", sc + 1)
                    if sc >= 1:
                        b.w(e, "s1kv", sc)
                    e.tensor_scalar(ktm[:, 0, 0:512], p.P[0][:, 0:512],
                                    0.0, 1e-3, ALU.max, ALU.add)
                    ins = e.tensor_scalar(ktm[:, 0, 512:1024], p.P[1][:, 0:512],
                                          0.0, 1e-3, ALU.max, ALU.add)
                    b.inc(ins, "s1ke")
                b.v(vphi)
                def sv(e, sc=sc):
                    b.w(e, "s1km", sc + 1)
                    if sc >= 1:
                        b.w(e, "s1kv", sc)
                    e.activation(vtm[:, 0, 0:512], p.P[2][:, 0:512], AF.Copy)
                    ins = e.activation(vtm[:, 0, 512:1024], p.P[3][:, 0:512],
                                       AF.Copy)
                    b.inc(ins, "s1ke")
                b.s(sv)
                def tkv(e, sc=sc):
                    b.w(e, "s1ke", 2 * (sc + 1))
                    ins = None
                    for h in range(16):
                        pb = (h % 2) * 64
                        j = h // 2
                        e.matmul(p.PAUX[pb:pb + 64, 64 * j:64 * j + 64],
                                 ktm[:, 0, 64 * h:64 * h + 64],
                                 vtm[:, 0, 64 * h:64 * h + 64],
                                 start=(sc == 0), stop=(sc == 15))
                        ins = e.matmul(p.PAUX[pb:pb + 64, 512 + j:512 + j + 1],
                                       ktm[:, 0, 64 * h:64 * h + 64],
                                       ones_col[:, 0:1],
                                       start=(sc == 0), stop=(sc == 15))
                    b.inc(ins, "s1kv")
                b.t(tkv)
        p.retire_sems(keep=("s1wl", "s1wc"))
        with phase(p, f"s1_kve{mb}") as b:
            def se(e):
                ins = e.activation(kvsb, p.PAUX[:, 0:512], AF.Copy)
                b.inc(ins, "s1ev")
            b.s(se)
            def ve(e):
                ins = e.tensor_copy(ksv[:, :], p.PAUX[:, 512:512 + DC])
                b.inc(ins, "s1ev")
            b.v(ve)
            def tfin(e):
                b.w(e, "s1ev", 2)
            b.t(tfin)
        p.retire_sems(keep=("s1wl", "s1wc"))
        for hm in range(2):
            hbase = base + 1024 * hm
            with phase(p, f"s1_q{mb}_{hm}") as b:
                nt = 0
                for m in range(DC):
                    for t2 in range(2):
                        i = nt; nt += 1
                        bank = i % 4
                        to = 1024 * hm + 512 * t2
                        def tq2(e, m=m, to=to, bank=bank, i=i):
                            if i == 0:
                                b.w(e, "s1wl", 384)
                            if i >= 4:
                                b.w(e, "s1qe", i - 3)
                            ins = None
                            for kc in range(DC):
                                ins = e.matmul(p.P[bank][:, 0:512],
                                               wres[:, 0, kc, 128 * m:128 * (m + 1)],
                                               ln_mb[:, kc, to:to + 512],
                                               start=(kc == 0), stop=(kc == DC - 1))
                            b.inc(ins, "s1qm")
                        b.t(tq2)
                        def vq(e, m=m, t2=t2, bank=bank, i=i):
                            b.w(e, "s1qm", i + 1)
                            ins = e.tensor_scalar(scr1, p.P[bank][:, 0:512],
                                                  bq_pp[:, m:m + 1], 0.0,
                                                  ALU.add, ALU.max)
                            b.vsync(e, ins)
                            ins = e.tensor_scalar(phq[:, m, 512 * t2:512 * t2 + 512],
                                                  scr1, 1e-3, None, ALU.add)
                            b.inc(ins, "s1qe")
                        b.v(vq)
            p.retire_sems(keep=("s1wl", "s1wc"))
            with phase(p, f"s1_at{mb}_{hm}") as b:
                nt = 0
                for j in range(DC):
                    for t2 in range(2):
                        i = nt; nt += 1
                        bank = i % 4
                        dslot = 512 * (i % 2)
                        def tn(e, j=j, t2=t2, bank=bank, i=i, dslot=dslot):
                            if i >= 4:
                                b.w(e, "s1ae", 2 * (i - 3))
                            if i >= 2:
                                b.w(e, "s1az", i - 1)
                            for pb in (0, 64):
                                e.matmul(p.P[bank][pb:pb + 64, 0:512],
                                         kvsb[pb:pb + 64, 64 * j:64 * j + 64],
                                         phq[pb:pb + 64, j, 512 * t2:512 * t2 + 512],
                                         start=True, stop=True)
                            ins = None
                            for pb in (0, 64):
                                ins = e.matmul(p.PAUX[pb:pb + 1, dslot:dslot + 512],
                                               ksv[pb:pb + 64, j:j + 1],
                                               phq[pb:pb + 64, j, 512 * t2:512 * t2 + 512],
                                               start=True, stop=True)
                            b.inc(ins, "s1am")
                        b.t(tn)
                        def vz(e, j=j, t2=t2, bank=bank, i=i, dslot=dslot):
                            b.w(e, "s1am", i + 1)
                            dr = denr[i % 2]
                            e.tensor_copy(dr[0:1, 0:512],
                                          p.PAUX[0:1, dslot:dslot + 512])
                            ins = e.tensor_copy(dr[0:1, 512:1024],
                                                p.PAUX[64:65, dslot:dslot + 512])
                            b.vsync(e, ins)
                            ins = e.reciprocal(dr[0:1, 0:1024], dr[0:1, 0:1024])
                            b.inc(ins, "s1az")
                        b.v(vz)
                        def trep(e, j=j, t2=t2, bank=bank, i=i):
                            b.w(e, "s1az", i + 1)
                            if i >= 2:
                                b.w(e, "s1ae", 2 * (i - 1))
                            dr = denr[i % 2]
                            pslot = 512 * (i % 2)
                            e.matmul(p.PREP[0:64, pslot:pslot + 512], ones_f[0:1, 0:64],
                                     dr[0:1, 0:512], start=True, stop=True)
                            ins = e.matmul(p.PREP[64:128, pslot:pslot + 512],
                                           ones_f[0:1, 0:64], dr[0:1, 512:1024],
                                           start=True, stop=True)
                            b.inc(ins, "s1ar")
                        b.t(trep)
                        def sn(e, j=j, t2=t2, bank=bank, i=i):
                            b.w(e, "s1ar", i + 1)
                            ins = e.activation(phq[:, j, 512 * t2:512 * t2 + 512],
                                               p.P[bank][:, 0:512], AF.Copy)
                            b.inc(ins, "s1ac")
                        b.s(sn)
                        def va(e, j=j, t2=t2, bank=bank, i=i):
                            b.w(e, "s1ac", i + 1)
                            pslot = 512 * (i % 2)
                            sl = phq[:, j, 512 * t2:512 * t2 + 512]
                            ins = e.tensor_tensor(sl, sl, p.PREP[:, pslot:pslot + 512],
                                                  ALU.mult)
                            b.inc(ins, "s1ae", 2)
                        b.v(va)
            p.retire_sems(keep=("s1wl", "s1wc"))
            with phase(p, f"s1_o{mb}_{hm}") as b:
                nt = 0
                for m in range(DC):
                    for t2 in range(2):
                        i = nt; nt += 1
                        bank = i % 4
                        slot = i % 2
                        off = hbase + 512 * t2
                        def yx(e, m=m, off=off, i=i, slot=slot):
                            if i >= 2:
                                b.w(e, "s1os", 32 * (i - 1))
                            b.dma(e, "s1xl", xts[:, slot, :],
                                  xin[128 * m:128 * (m + 1), off:off + 512])
                        b.y(yx)
                        def to(e, m=m, t2=t2, bank=bank, i=i):
                            if i == 0 and mb == 0 and hm == 0:
                                b.w(e, "s1wl", 512)
                            if i >= 4:
                                b.w(e, "s1oe", i - 3)
                            ins = None
                            for kc in range(DC):
                                ins = e.matmul(p.P[bank][:, 0:512],
                                               wres[:, 3, kc, 128 * m:128 * (m + 1)],
                                               phq[:, kc, 512 * t2:512 * t2 + 512],
                                               start=(kc == 0), stop=(kc == DC - 1))
                            b.inc(ins, "s1om")
                        b.t(to)
                        def vo(e, m=m, off=off, bank=bank, i=i, slot=slot):
                            b.w(e, "s1om", i + 1)
                            b.w(e, "s1xl", 16 * (i + 1))
                            if i >= 2:
                                b.w(e, "s1os", 32 * (i - 1))
                            a = lnx[:, slot, :]
                            c = xrs[:, slot, :]
                            xt = xts[:, slot, :]
                            ins = e.tensor_tensor(a, xt, p.smu[:, off:off + 512],
                                                  ALU.subtract)
                            b.vsync(e, ins)
                            ins = e.tensor_tensor(a, a, p.sr[:, off:off + 512],
                                                  ALU.mult)
                            b.vsync(e, ins)
                            ins = e.scalar_tensor_tensor(c, p.P[bank][:, 0:512],
                                                         bo_pp[:, m:m + 1], a,
                                                         ALU.add, ALU.add)
                            b.vsync(e, ins)
                            ins = e.tensor_tensor(xt, c, xt, ALU.add)
                            b.inc(ins, "s1oe")
                        b.v(vo)
                        def yo(e, m=m, off=off, i=i, slot=slot):
                            b.w(e, "s1oe", i + 1)
                            b.dma(e, "s1os", xres_d[128 * m:128 * (m + 1), off:off + 512],
                                  xrs[:, slot, :])
                            b.dma(e, "s1os", xmid_d[128 * m:128 * (m + 1), off:off + 512],
                                  xts[:, slot, :])
                        b.y(yo)
                def yfin(e):
                    b.w(e, "s1os")
                b.y(yfin)
            p.retire_sems(keep=("s1wl", "s1wc"))

    emit_stats(p, "s1b", xres_d, ntok, eps=LNEPS1)
    p.retire_sems(keep=("s1wc",))

    w1res = p.bfv(p.bf1, 0, 32768, DC)
    with phase(p, "s1_w1s") as b:
        def sc1(e):
            b.w(e, "s1wc", 64)
            for blk in range(4):
                for k in range(DC):
                    b.dma(e, "s1w1", w1res[:, k, 1024 * blk:1024 * (blk + 1)],
                          w1_bf[128 * k:128 * (k + 1),
                                1024 * blk:1024 * (blk + 1)])
        b.s(sc1)

    ln2x = p.bfv(p.bf2, 0, 4096, DC)
    hbuf = p.bfv(p.bf2, 4096, 16384, 32)
    w2sl = p.bfv(p.bf2, 20480, 2048, 2)
    xmf = p.f1[:, 0:4096].rearrange("p (c t) -> p c t", c=DC)
    acf = None
    for i in range(8):
        off = 512 * i
        emit_ln(p, f"s1n{i}", xres_d, ln2x, [(off, 512)], buf_base=off,
                gb=(g2v, b2v))
        with phase(p, f"s1_m{i}") as b:
            acf = [p.P[0][:, 0:512], p.P[1][:, 0:512], p.P[2][:, 0:512],
                   p.P[3][:, 0:512], p.PREP[:, 0:512], p.PREP[:, 512:1024],
                   p.PAUX[:, 0:512], p.PAUX[:, 512:1024]]
            def scx(e, i=i):
                if i >= 1:
                    b.w(e, "s1ys", 16 * i)
                b.dma(e, "s1xm", xmf[:, :, :], x_ap(xmid_d, off, 512))
            b.s(scx)
            def sy2(e, i=i):
                for ki in range(32):
                    s_ = 32 * i + ki
                    if s_ == 0:
                        b.w(e, "s1wc", 128)
                    if s_ >= 2:
                        b.w(e, "s1dk", s_ - 1)
                    b.dma(e, "s1w2", w2sl[:, s_ % 2, :],
                          w2_bf[128 * ki:128 * (ki + 1), :])
                b.w(e, "s1yv", 8 * (i + 1))
                b.dma(e, "s1ys", x_ap(xout, off, 512), xmf[:, :, :])
            b.y(sy2)
            for m in range(32):
                def tw1(e, i=i, m=m):
                    if m == 0 and i >= 1:
                        b.w(e, "s1yv", 8 * i)
                    if m % 8 == 0:
                        b.w(e, "s1w1", 128 * (m // 8 + 1))
                    if m >= 4:
                        b.w(e, "s1h", 32 * i + m - 3)
                    ins = None
                    for k in range(DC):
                        ins = e.matmul(p.P[m % 4][:, 0:512],
                                       w1res[:, k, 128 * m:128 * (m + 1)],
                                       ln2x[:, k, :],
                                       start=(k == 0), stop=(k == DC - 1))
                    b.inc(ins, "s1hm")
                b.t(tw1)
                def sh(e, i=i, m=m):
                    b.w(e, "s1hm", 32 * i + m + 1)
                    ins = e.activation(hbuf[:, m, :], p.P[m % 4][:, 0:512],
                                       AF.Gelu_apprx_tanh,
                                       bias=b1_pp[:, m:m + 1])
                    b.inc(ins, "s1h")
                b.s(sh)
            def tw2(e, i=i):
                b.w(e, "s1h", 32 * (i + 1))
                for ki in range(32):
                    s_ = 32 * i + ki
                    b.w(e, "s1w2", 16 * (s_ + 1))
                    ins = None
                    for m2 in range(DC):
                        ins = e.matmul(acf[m2],
                                       w2sl[:, s_ % 2, 128 * m2:128 * (m2 + 1)],
                                       hbuf[:, ki, :],
                                       start=(ki == 0), stop=(ki == 31))
                    b.inc(ins, "s1dk")
            b.t(tw2)
            def vy2(e, i=i):
                b.w(e, "s1dk", 32 * (i + 1))
                b.w(e, "s1xm", 16 * (i + 1))
                for m2 in range(DC):
                    ins = e.scalar_tensor_tensor(xmf[:, m2, :], acf[m2],
                                                 b2_pp[:, m2:m2 + 1],
                                                 xmf[:, m2, :],
                                                 ALU.add, ALU.add)
                    b.inc(ins, "s1yv")
            b.v(vy2)
        p.retire_sems(keep=("s1wc", "s1w1", "s1w2", "s1dk", "s1xm",
                            "s1ys", "s1yv", "s1h", "s1hm"))
    with phase(p, "s1_fin") as b:
        def vf(e):
            b.w(e, "s1ys")
        b.v(vf)


_PROGS = {}


def _get_prog(which):
    if which not in _PROGS:
        if which == "A":
            _PROGS[which] = build_launch_A(("s1", "s2", "s3", "s4"))
        else:
            _PROGS[which] = build_launch_B(("s5", "s6", "s7", "s8"))
    return _PROGS[which]


def kernel(**inputs):
    x0 = np.asarray(inputs["L0_attn_a_inputs"], dtype=np.float32)

    pa = _get_prog("A")
    in_a = prep_inputs_A(inputs, x0)
    res_a = run_bass_kernel_spmd(pa.nc, in_a, core_ids=list(range(NCORES)))
    x4 = finish_outputs_A(res_a.results)

    pb = _get_prog("B")
    in_b = prep_inputs_B(inputs, x4)
    res_b = run_bass_kernel_spmd(pb.nc, in_b, core_ids=list(range(NCORES)))
    return finish_outputs_B(res_b.results)
